# version 05
# baseline (speedup 1.0000x reference)
import numpy as np
import ml_dtypes
import concourse.bass as bass
import concourse.mybir as mybir
from concourse.bass_utils import run_bass_kernel_spmd
from contextlib import ExitStack

F32 = mybir.dt.float32
BF16 = mybir.dt.bfloat16
I32 = mybir.dt.int32
AF = mybir.ActivationFunctionType
ALU = mybir.AluOpType
AX = mybir.AxisListType
BF = ml_dtypes.bfloat16


class Cfg:
    def __init__(self, T, TILE, D, NKH, HID):
        self.T, self.TILE, self.D, self.NKH, self.HID = T, TILE, D, NKH, HID
        self.NVH = 2 * NKH
        self.NF = D // 128
        self.NT = T // TILE
        self.NCH = TILE // 128
        self.TQ = T // 4
        self.QW = NKH * 128
        self.VW = self.NVH * 128
        self.ABW = 2 * self.NVH
        self.WC = 2 * self.QW + self.VW
        self.WCOLS = self.WC + self.VW + self.ABW
        self.OUTC = D // 128
        self.NHC = HID // 128
        self.NHB = HID // 512
        self.GSZ = 4
        self.NGRP = (self.NHC + self.GSZ - 1) // self.GSZ
        self.NLEV = 4
        self.NQB = self.TQ // 128
        self.NTB = T // 128
        self.W2 = min(512, TILE)
        self.W4 = min(512, self.TQ)
        self.NZC = self.VW // 512 if self.VW >= 512 else 0
        self.ZW = 512 if self.VW >= 512 else self.VW
        if self.NZC == 0:
            self.NZC = 1


REAL = Cfg(T=4096, TILE=512, D=2048, NKH=4, HID=5632)
TINY = Cfg(T=512, TILE=128, D=256, NKH=2, HID=512)


class K:
    ENGS = ("tensor", "vector", "scalar", "gpsimd", "sync")
    ESEM = {"tensor": "s_pe", "vector": "s_dve", "scalar": "s_act",
            "gpsimd": "s_gp", "sync": "s_sp"}

    def __init__(self):
        self.streams = {e: [] for e in K.ENGS}
        self.count = {}
        self.mark = {e: {} for e in K.ENGS}
        self.writer = {}
        self.readers = {}
        self.dma_pend = {}
        self.sem_names = set(K.ESEM.values())

    def _wait(self, eng, sem, val):
        if val <= 0:
            return
        if self.mark[eng].get(sem, 0) >= val:
            return
        self.mark[eng][sem] = val
        self.streams[eng].append(("w", sem, val))

    def _deps(self, eng, reads, writes):
        for k in reads:
            for s, v in self.writer.get(k, {}).items():
                self._wait(eng, s, v)
        for k in writes:
            for s, v in self.writer.get(k, {}).items():
                self._wait(eng, s, v)
            for s, v in self.readers.get(k, {}).items():
                self._wait(eng, s, v)

    def _post(self, sem, v, reads, writes):
        for k in writes:
            self.writer.setdefault(k, {})[sem] = v
            self.readers[k] = {}
        for k in reads:
            self.readers.setdefault(k, {})[sem] = v

    PSUM_KEYS = frozenset(
        ["PP0", "PP1", "PA", "PB", "PC", "PD", "PF0", "PF1"])

    def _excl(self, reads, writes):
        pr = [k for k in reads if k in K.PSUM_KEYS]
        if pr:
            reads = [k for k in reads if k not in K.PSUM_KEYS]
            writes = list(writes) + pr
        return reads, writes

    def emit(self, eng, fns, reads=(), writes=()):
        if not isinstance(fns, (list, tuple)):
            fns = [fns]
        reads, writes = self._excl(reads, writes)
        self._deps(eng, reads, writes)
        sem = K.ESEM[eng]
        self.count[sem] = self.count.get(sem, 0) + 1
        v = self.count[sem]
        for i, fn in enumerate(fns):
            self.streams[eng].append(
                ("i", fn, sem if i == len(fns) - 1 else None, 1))
        self._post(sem, v, reads, writes)
        return v

    def dma(self, eng, sem, fns, reads=(), writes=()):
        if not isinstance(fns, (list, tuple)):
            fns = [fns]
        reads, writes = self._excl(reads, writes)
        self.sem_names.add(sem)
        if self.dma_pend.get(sem):
            self._wait(eng, sem, self.count.get(sem, 0))
        self._deps(eng, reads, writes)
        for fn in fns:
            self.count[sem] = self.count.get(sem, 0) + 16
            self.streams[eng].append(("i", fn, sem, 16))
        v = self.count[sem]
        self.dma_pend[sem] = True
        self._post(sem, v, reads, writes)
        return v

    def fence(self):
        for eng in K.ENGS:
            own = K.ESEM[eng]
            for sem, cnt in list(self.count.items()):
                if sem != own:
                    self._wait(eng, sem, cnt)

    def replay(self, nc, semh):
        with nc.Block() as block:
            for name in K.ENGS:
                stream = self.streams[name]
                if not stream:
                    continue

                def body(eng, stream=stream):
                    for it in stream:
                        if it[0] == "w":
                            eng.wait_ge(semh[it[1]], it[2])
                        else:
                            bi = it[1](eng)
                            if it[2] is not None:
                                bi.then_inc(semh[it[2]], it[3])
                getattr(block, name)(body)


def build(cfg, upto=5, p2lvl=10**9):
    c = cfg
    nc = bass.Bass("TRN2", target_bir_lowering=False, debug=False)
    k = K()

    dp = nc.declare_dram_parameter
    xfull = dp("xfull", [c.T, c.D], F32, isOutput=False)
    xq = dp("xq", [c.TQ, c.D], F32, isOutput=False)
    xqT = dp("xqT", [c.D, c.TQ], F32, isOutput=False)
    wcomb = dp("wcomb", [c.D, c.WCOLS], F32, isOutput=False)
    convw = dp("convw", [c.WC, 4], F32, isOutput=False)
    alog = dp("alog", [c.NVH, 1], F32, isOutput=False)
    dtb = dp("dtb", [c.NVH, 1], F32, isOutput=False)
    normw4 = dp("normw4", [1, 512], F32, isOutput=False)
    wout = dp("wout", [c.VW, c.D], F32, isOutput=False)
    wgT = dp("wgT", [c.D, c.HID], I32, isOutput=False)
    wuT = dp("wuT", [c.D, c.HID], I32, isOutput=False)
    wdT = dp("wdT", [c.HID, c.D], I32, isOutput=False)
    sg2 = dp("sg2", [128, c.NHC], F32, isOutput=False)
    su2 = dp("su2", [128, c.NHC], F32, isOutput=False)
    sd2 = dp("sd2", [128, c.OUTC], F32, isOutput=False)
    eyebf = dp("eyebf", [128, 128], BF16, isOutput=False)
    eyef = dp("eyef", [128, 128], F32, isOutput=False)
    ltri = dp("ltri", [128, 128], F32, isOutput=False)
    maskE = dp("maskE", [128, 128], F32, isOutput=False)
    mask2 = dp("mask2", [128, 128], F32, isOutput=False)
    mask01 = dp("mask01", [128, 128], F32, isOutput=False)
    rowsel = dp("rowsel", [c.NVH, 128 * c.NVH], F32, isOutput=False)
    outfm = dp("outfm", [c.D, c.TQ], F32, isOutput=True)

    xn_sc = nc.dram_tensor("xn_sc", [c.T, c.D], BF16)
    xnq_sc = nc.dram_tensor("xnq_sc", [c.TQ, c.D], BF16)
    rsin = nc.dram_tensor("rsin", [4, c.D, c.TQ], F32)
    gu_sc = nc.dram_tensor("gu_sc", [c.NHC * 128, c.TQ], BF16)
    rsout = nc.dram_tensor("rsout", [c.D, c.TQ], F32)

    st = ExitStack()
    sb = lambda n, s, d: st.enter_context(nc.sbuf_tensor(n, s, d))
    ps = lambda n, s, d: st.enter_context(nc.psum_tensor(n, s, d))

    PP0 = ps("PP0", [128, 512], F32)
    PP1 = ps("PP1", [128, 512], F32)
    PA = ps("PA", [128, 512], F32)
    PB = ps("PB", [128, 512], F32)
    PC = ps("PC", [128, 512], F32)
    PD = ps("PD", [128, 512], F32)
    PF0 = ps("PF0", [128, 1024], BF16)
    PF1 = ps("PF1", [128, 1024], BF16)

    s_eyebf = sb("s_eyebf", [128, 128], BF16)
    s_eyef = sb("s_eyef", [128, 128], F32)
    s_ltri = sb("s_ltri", [128, 128], F32)
    s_maskE = sb("s_maskE", [128, 128], F32)
    s_mask2 = sb("s_mask2", [128, 128], F32)
    s_mask01 = sb("s_mask01", [128, 128], F32)
    s_normw = sb("s_normw", [1, 512], F32)
    s_alog = sb("s_alog", [c.NVH, 1], F32)
    s_dtb = sb("s_dtb", [c.NVH, 1], F32)
    s_nega = sb("s_nega", [c.NVH, 1], F32)
    s_cst = sb("s_cst", [128, 8], F32)
    s_ones1 = sb("s_ones1", [1, 128], F32)
    s_onesc = sb("s_onesc", [128, 1], F32)
    s_onesb = sb("s_onesb", [128, 1], BF16)
    s_m128 = sb("s_m128", [1, 128], BF16)
    s_normb = sb("s_normb", [128, 512], F32)
    s_onesq = sb("s_onesq", [128, 128], F32)
    s_rsel = sb("s_rsel", [c.NVH, 128 * c.NVH], F32)
    s_convw = sb("s_convw", [128, 4 * (c.WC // 128)], F32)
    s_sg2 = sb("s_sg2", [128, c.NHC], F32)
    s_su2 = sb("s_su2", [128, c.NHC], F32)
    s_sd2 = sb("s_sd2", [128, c.OUTC], F32)

    CST = {1e-6: 0, 128.0: 1, 128e-6: 2, 1.0 / c.D: 3, 1e-5: 4,
           1.0 / 128: 5, 1.0: 6, -1.0: 7}

    def cst(v, p=128):
        return s_cst[0:p, CST[v]:CST[v] + 1]

    k.dma("sync", "d_c", [
        (lambda e, o=o_, i=i_: e.dma_start(out=o, in_=i))
        for o_, i_ in [
            (s_eyebf[:, :], eyebf[:, :]), (s_eyef[:, :], eyef[:, :]),
            (s_ltri[:, :], ltri[:, :]), (s_maskE[:, :], maskE[:, :]),
            (s_mask2[:, :], mask2[:, :]), (s_mask01[:, :], mask01[:, :]),
            (s_normw[:, :], normw4[:, :]), (s_alog[:, :], alog[:, :]),
            (s_dtb[:, :], dtb[:, :]), (s_sg2[:, :], sg2[:, :]),
            (s_su2[:, :], su2[:, :]), (s_sd2[:, :], sd2[:, :]),
            (s_rsel[:, :], rowsel[:, :]),
        ] + [(s_convw[:, 4 * f:4 * f + 4], convw[128 * f:128 * (f + 1), :])
             for f in range(c.WC // 128)]
    ], writes=["consts"])
    for val, col in CST.items():
        k.emit("vector", (lambda e, v=val, cl=col: e.memset(
            s_cst[:, cl:cl + 1], v)), writes=["consts"])
    k.emit("vector", lambda e: e.memset(s_ones1[:, :], 1.0), writes=["consts"])
    k.emit("vector", lambda e: e.memset(s_onesc[:, :], 1.0), writes=["consts"])
    k.emit("vector", lambda e: e.memset(s_onesb[:, :], 1.0), writes=["consts"])
    k.emit("vector", lambda e: e.memset(s_m128[:, :], -128.0), writes=["consts"])
    k.emit("vector", lambda e: e.memset(s_onesq[:, :], 1.0), writes=["consts"])
    k.emit("scalar", lambda e: e.activation(s_nega[:, :], s_alog[:, :], AF.Exp),
           reads=["consts"], writes=["nega"])
    k.emit("vector", lambda e: e.tensor_scalar(
        s_nega[:, :], s_nega[:, :], -1.0, None, ALU.mult),
        reads=["nega"], writes=["nega"])
    k.emit("tensor", lambda e: e.matmul(
        PA[:, 0:512], s_ones1[:, :], s_normw[:, :], start=True, stop=True),
        reads=["consts"], writes=["PA"])
    k.emit("vector", lambda e: e.tensor_copy(s_normb[:, :], PA[:, 0:512]),
           reads=["PA"], writes=["normb"])

    stP1 = ExitStack()
    xl = [stP1.enter_context(nc.sbuf_tensor(f"xln{i}", [128, c.D], F32))
          for i in range(2)]
    xb = [stP1.enter_context(nc.sbuf_tensor(f"xnb{i}", [128, c.D], BF16))
          for i in range(2)]
    dumpb = stP1.enter_context(nc.sbuf_tensor("dumpb", [128, c.D], F32))
    lncol = stP1.enter_context(nc.sbuf_tensor("lncol", [128, 4], F32))

    def ln_block(src_ap, dst_ap, p, keys):
        xk, bk = f"xl{p}", f"xb{p}"
        xlp, xbp = xl[p], xb[p]
        k.dma("sync", f"d_x{p}", (lambda e: e.dma_start(
            out=xlp[:, :], in_=src_ap)), writes=[xk])
        k.emit("vector", lambda e: e.reduce_sum(
            lncol[:, 0:1], xlp[:, :], axis=AX.X),
            reads=[xk], writes=["lncol"])
        k.emit("scalar", lambda e: e.activation(
            lncol[:, 1:2], lncol[:, 0:1], AF.Copy, scale=cst(1.0 / c.D)),
            reads=["lncol", "consts"], writes=["lncol"])
        k.emit("vector", lambda e: e.tensor_scalar(
            xlp[:, :], xlp[:, :], lncol[:, 1:2], None, ALU.subtract),
            reads=[xk, "lncol"], writes=[xk])
        k.emit("scalar", lambda e: e.activation(
            dumpb[:, :], xlp[:, :], AF.Square, accum_out=lncol[:, 2:3]),
            reads=[xk], writes=["dumpb", "lncol"])
        k.emit("scalar", lambda e: e.activation(
            lncol[:, 3:4], lncol[:, 2:3], AF.Sqrt,
            scale=cst(1.0 / c.D), bias=cst(1e-5)),
            reads=["lncol", "consts"], writes=["lncol"])
        k.emit("vector", lambda e: e.reciprocal(
            lncol[:, 3:4], lncol[:, 3:4]),
            reads=["lncol"], writes=["lncol"])
        k.emit("vector", lambda e: e.tensor_scalar(
            xbp[:, :], xlp[:, :], lncol[:, 3:4], None, ALU.mult),
            reads=[xk, "lncol"], writes=[bk])
        k.dma("sync", f"d_xw{p}", (lambda e: e.dma_start(
            out=dst_ap, in_=xbp[:, :])), reads=[bk], writes=keys)

    for tb in range(c.NTB):
        ln_block(xfull[128 * tb:128 * (tb + 1), :],
                 xn_sc[128 * tb:128 * (tb + 1), :], tb % 2, ["xn_sc"])
    for qb in range(c.NQB):
        ln_block(xq[128 * qb:128 * (qb + 1), :],
                 xnq_sc[128 * qb:128 * (qb + 1), :], qb % 2, ["xnq_sc"])
    k.fence()
    stP1.close()

    def finish_dummy():
        sfin = st.enter_context(nc.sbuf_tensor("s_dummy", [128, c.TQ], F32))
        k.emit("vector", lambda e: e.memset(sfin[:, :], 0.0), writes=["dumz"])
        k.dma("sync", "d_out", [
            (lambda e, o=o: e.dma_start(
                out=outfm[128 * o:128 * (o + 1), :], in_=sfin[:, :]))
            for o in range(c.OUTC)], reads=["dumz"], writes=["outz"])
        k._wait("gpsimd", "d_out", k.count["d_out"])
        semh = {}
        for name in sorted(set(list(k.count.keys()) + list(K.ESEM.values()))):
            semh[name] = st.enter_context(nc.semaphore(name))
        k.replay(nc, semh)
        st.close()
        return nc

    if upto <= 1:
        return finish_dummy()

    QKV_OC = c.WC // 128
    NWG = (QKV_OC + 3) // 4

    stP2 = ExitStack()
    sbp = lambda n, s, d: stP2.enter_context(nc.sbuf_tensor(n, s, d))
    s_xnf = [sbp(f"s_xnf{f}", [128, c.TILE], BF16) for f in range(c.NF)]
    s_wg = [[sbp(f"s_wg{g}_{f}", [128, 512], BF16) for f in range(c.NF)]
            for g in range(2)]
    s_wz = [[sbp(f"s_wz{z}_{f}", [128, c.ZW], BF16) for f in range(c.NF)]
            for z in range(c.NZC)]
    s_wab = [sbp(f"s_wab{f}", [128, c.ABW], BF16) for f in range(c.NF)]
    s_projb = sbp("s_projb", [128, QKV_OC * (c.TILE + 3)], BF16)
    s_qkvs = sbp("s_qkvs", [128, QKV_OC * c.TILE], BF16)
    s_cacc = sbp("s_cacc", [128, c.TILE], F32)
    s_zsil = sbp("s_zsil", [128, c.NCH * c.VW], BF16)
    s_zg = sbp("s_zg", [128, c.VW], F32)
    s_gc = sbp("s_gc", [c.NVH, c.TILE], F32)
    s_bc = sbp("s_bc", [c.NVH, c.TILE], F32)
    s_gtm = sbp("s_gtm", [128, c.NVH], F32)
    s_btm = sbp("s_btm", [128, c.NVH], F32)
    s_G = sbp("s_G", [128, c.NVH], F32)
    s_Gfm = sbp("s_Gfm", [c.NVH, 128], F32)
    s_gam = sbp("s_gam", [128, 4 * c.NVH], F32)
    s_nbf = sbp("s_nbf", [c.NVH, c.TILE], F32)
    s_sks = sbp("s_sks", [1, 256], F32)
    s_ktf = sbp("s_ktf", [128, 128], BF16)
    s_qtf = sbp("s_qtf", [128, 128], BF16)
    s_ktm = sbp("s_ktm", [128, 128], BF16)
    s_ksq = sbp("s_ksq", [128, 256], F32)
    s_argE = sbp("s_argE", [128, 128], F32)
    s_argN = sbp("s_argN", [128, 128], F32)
    s_E = sbp("s_E", [128, 128], F32)
    s_E2 = sbp("s_E2", [128, 128], F32)
    s_es = sbp("s_es", [128, 128], F32)
    s_ntm = sbp("s_ntm", [128, 128], F32)
    s_NT = sbp("s_NT", [128, 128], BF16)
    s_N = sbp("s_N", [128, 128], BF16)
    s_TT = [sbp(f"s_TT{i}", [128, 128], BF16) for i in range(2)]
    s_P = [sbp(f"s_P{i}", [128, 128], BF16) for i in range(2)]
    s_PT = [sbp(f"s_PT{i}", [128, 128], BF16) for i in range(2)]
    s_A2T = sbp("s_A2T", [128, 128], BF16)
    s_RHS = sbp("s_RHS", [128, 128], BF16)
    s_Xbf = sbp("s_Xbf", [128, 128], BF16)
    s_Xw = sbp("s_Xw", [128, 128], BF16)
    s_Otm = sbp("s_Otm", [128, 128], F32)
    s_dump = sbp("s_dump", [128, 128], F32)
    s_oc2 = sbp("s_oc2", [128, 2], F32)
    s_S = [sbp(f"s_S{h}", [128, 128], F32) for h in range(c.NVH)]
    s_Sbf = [sbp(f"s_Sbf{h}", [128, 128], BF16) for h in range(c.NVH)]
    s_ofm = [sbp(f"s_ofm{v}", [128, c.TILE], BF16) for v in range(c.NVH)]
    s_sfm = [sbp(f"s_sfm{i}", [128, 512], F32) for i in range(2)]
    s_wout = [sbp(f"s_wout{v}", [128, c.D], BF16) for v in range(c.NVH)]
    k.dma("gpsimd", "d_wo", [
        (lambda e, v=v: e.dma_start(out=s_wout[v][:, :],
                                    in_=wout[128 * v:128 * (v + 1), :]))
        for v in range(c.NVH)], writes=["woutsb"])

    for h in range(c.NVH):
        k.emit("vector", (lambda e, h=h: e.memset(s_S[h][:, :], 0.0)),
               writes=[f"S{h}"])
        k.emit("vector", (lambda e, h=h: e.memset(s_Sbf[h][:, :], 0.0)),
               writes=[f"Sbf{h}"])
    for oc in range(QKV_OC):
        k.emit("vector", (lambda e, oc=oc: e.memset(
            s_projb[:, oc * (c.TILE + 3):oc * (c.TILE + 3) + 3], 0.0)),
            writes=[f"projb{oc}"])

    def load_xnf(t):
        k.dma("sync", "d_xnr", [
            (lambda e, f=f, t=t: e.dma_start(
                out=s_xnf[f][:, :],
                in_=xn_sc[t * c.TILE:(t + 1) * c.TILE, 128 * f:128 * (f + 1)],
                transpose=True))
            for f in range(c.NF)],
            reads=["xn_sc"], writes=[f"xnf{f}" for f in range(c.NF)])

    def load_wg(g, par):
        ocs = list(range(4 * g, min(4 * g + 4, QKV_OC)))
        k.dma("gpsimd", f"d_wq{par}", [
            (lambda e, f=f, oc=oc, j=j, par=par: e.dma_start(
                out=s_wg[par][f][:, 128 * j:128 * (j + 1)],
                in_=wcomb[128 * f:128 * (f + 1), 128 * oc:128 * (oc + 1)]))
            for f in range(c.NF) for j, oc in enumerate(ocs)],
            writes=[f"wg{par}"])

    def load_wz(z):
        k.dma("gpsimd", f"d_wz{z}", [
            (lambda e, f=f, z=z: e.dma_start(
                out=s_wz[z][f][:, :],
                in_=wcomb[128 * f:128 * (f + 1),
                          c.WC + c.ZW * z:c.WC + c.ZW * (z + 1)]))
            for f in range(c.NF)], writes=[f"wz{z}"])

    def load_wab():
        k.dma("gpsimd", "d_wab", [
            (lambda e, f=f: e.dma_start(
                out=s_wab[f][:, :],
                in_=wcomb[128 * f:128 * (f + 1), c.WC + c.VW:c.WCOLS]))
            for f in range(c.NF)], writes=["wab"])

    PBANK = [PP0, PP1, PA, PB]
    PBK = ["PP0", "PP1", "PA", "PB"]
    xnf_keys = [f"xnf{f}" for f in range(c.NF)]

    load_xnf(0)
    load_wg(0, 0)
    if NWG > 1:
        load_wg(1, 1)
    for z in range(c.NZC):
        load_wz(z)
    load_wab()

    for t in range(c.NT):
        for g in range(NWG):
            par = g % 2
            ocs = list(range(4 * g, min(4 * g + 4, QKV_OC)))
            for j, oc in enumerate(ocs):
                bank, bkey = PBANK[j], PBK[j]
                ob = oc * (c.TILE + 3)
                k.emit("tensor", [
                    (lambda e, f=f, j=j, par=par, bank=bank: e.matmul(
                        bank[:, 0:c.TILE],
                        s_wg[par][f][:, 128 * j:128 * (j + 1)],
                        s_xnf[f][:, :],
                        start=(f == 0), stop=(f == c.NF - 1)))
                    for f in range(c.NF)],
                    reads=[f"wg{par}"] + xnf_keys, writes=[bkey])
                k.emit("scalar", (lambda e, bank=bank, ob=ob: e.activation(
                    s_projb[:, ob + 3:ob + 3 + c.TILE], bank[:, 0:c.TILE],
                    AF.Copy)),
                    reads=[bkey], writes=[f"projb{oc}"])
                k.emit("vector", (lambda e, oc=oc, ob=ob: e.tensor_scalar(
                    s_cacc[:, :], s_projb[:, ob:ob + c.TILE],
                    s_convw[:, 4 * oc:4 * oc + 1], None, ALU.mult)),
                    reads=[f"projb{oc}", "consts"], writes=["cacc"])
                for tap in range(1, 4):
                    k.emit("vector", (lambda e, oc=oc, ob=ob, tap=tap:
                        e.scalar_tensor_tensor(
                            s_cacc[:, :],
                            s_projb[:, ob + tap:ob + tap + c.TILE],
                            s_convw[:, 4 * oc + tap:4 * oc + tap + 1],
                            s_cacc[:, :], ALU.mult, ALU.add)),
                        reads=[f"projb{oc}", "consts", "cacc"],
                        writes=["cacc"])
                k.emit("scalar", (lambda e, oc=oc: e.activation(
                    s_qkvs[:, oc * c.TILE:(oc + 1) * c.TILE], s_cacc[:, :],
                    AF.Silu)),
                    reads=["cacc"], writes=[f"qkvs{oc}"])
                k.emit("vector", (lambda e, oc=oc, ob=ob: e.tensor_copy(
                    s_projb[:, ob:ob + 3],
                    s_projb[:, ob + c.TILE:ob + c.TILE + 3])),
                    reads=[f"projb{oc}"], writes=[f"projb{oc}"])
            if NWG > 1:
                ni = t * NWG + g + 2
                if ni < c.NT * NWG:
                    nig = ni % NWG
                    load_wg(nig, nig % 2)
        if p2lvl < 12:
            continue
        for cb in range(c.NCH):
            for z in range(c.NZC):
                zpar = (cb * c.NZC + z) % 2
                bank, bkey = (PP0, "PP0") if zpar == 0 else (PP1, "PP1")
                k.emit("tensor", [
                    (lambda e, f=f, cb=cb, z=z, bank=bank: e.matmul(
                        bank[:, 0:c.ZW],
                        s_xnf[f][:, 128 * cb:128 * (cb + 1)],
                        s_wz[z][f][:, :],
                        start=(f == 0), stop=(f == c.NF - 1)))
                    for f in range(c.NF)],
                    reads=[f"wz{z}"] + xnf_keys, writes=[bkey])
                k.emit("scalar", (lambda e, cb=cb, z=z, bank=bank: e.activation(
                    s_zsil[:, cb * c.VW + c.ZW * z:cb * c.VW + c.ZW * (z + 1)],
                    bank[:, 0:c.ZW], AF.Silu)),
                    reads=[bkey], writes=["zsil"])
        for h2 in range(c.TILE // c.W2):
            co = c.W2 * h2
            k.emit("tensor", [
                (lambda e, f=f, co=co: e.matmul(
                    PA[0:c.NVH, 0:c.W2], s_wab[f][:, 0:c.NVH],
                    s_xnf[f][:, co:co + c.W2],
                    start=(f == 0), stop=(f == c.NF - 1)))
                for f in range(c.NF)],
                reads=["wab"] + xnf_keys, writes=["PA"])
            k.emit("tensor", [
                (lambda e, f=f, co=co: e.matmul(
                    PB[0:c.NVH, 0:c.W2], s_wab[f][:, c.NVH:c.ABW],
                    s_xnf[f][:, co:co + c.W2],
                    start=(f == 0), stop=(f == c.NF - 1)))
                for f in range(c.NF)],
                reads=["wab"] + xnf_keys, writes=["PB"])
            k.emit("scalar", (lambda e, co=co: e.activation(
                s_gc[:, co:co + c.W2], PA[0:c.NVH, 0:c.W2], AF.Exp,
                bias=s_dtb[:, 0:1])),
                reads=["PA", "consts"], writes=["gbc"])
            k.emit("scalar", (lambda e, co=co: e.add(
                s_gc[:, co:co + c.W2], s_gc[:, co:co + c.W2], 1.0)),
                reads=["gbc", "consts"], writes=["gbc"])
            k.emit("scalar", (lambda e, co=co: e.activation(
                s_gc[:, co:co + c.W2], s_gc[:, co:co + c.W2],
                AF.Ln)), reads=["gbc"], writes=["gbc"])
            k.emit("vector", (lambda e, co=co: e.tensor_scalar(
                s_gc[:, co:co + c.W2], s_gc[:, co:co + c.W2],
                s_nega[:, 0:1], None, ALU.mult)),
                reads=["gbc", "nega"], writes=["gbc"])
            k.emit("scalar", (lambda e, co=co: e.activation(
                s_bc[:, co:co + c.W2], PB[0:c.NVH, 0:c.W2],
                AF.Sigmoid)), reads=["PB"], writes=["gbc"])
            k.emit("vector", (lambda e, co=co: e.tensor_scalar(
                s_nbf[:, co:co + c.W2], s_bc[:, co:co + c.W2],
                -1.0, None, ALU.mult)),
                reads=["gbc"], writes=["nbf"])
        if t < c.NT - 1:
            load_xnf(t + 1)
            for z in range(c.NZC):
                load_wz(z)
            load_wab()

        if p2lvl < 13:
            continue
        for cb in range(c.NCH):
            tok0 = cb * 128
            for z in range(c.NZC):
                k.emit("vector", (lambda e, cb=cb, z=z: e.tensor_tensor(
                    s_zg[:, c.ZW * z:c.ZW * (z + 1)],
                    s_zsil[:, cb * c.VW + c.ZW * z:cb * c.VW + c.ZW * (z + 1)],
                    s_normb[:, 0:c.ZW], ALU.mult)),
                    reads=["zsil", "normb"], writes=["zg"])
            k.emit("tensor", (lambda e, tok0=tok0: e.transpose(
                PB[0:128, 0:c.NVH], s_gc[:, tok0:tok0 + 128],
                s_eyef[0:c.NVH, 0:c.NVH])),
                reads=["gbc", "consts"], writes=["PB"])
            k.emit("tensor", (lambda e, tok0=tok0: e.transpose(
                PB[0:128, 64:64 + c.NVH], s_bc[:, tok0:tok0 + 128],
                s_eyef[0:c.NVH, 0:c.NVH])),
                reads=["gbc", "consts"], writes=["PB"])
            k.emit("vector", lambda e: e.tensor_copy(
                s_gtm[:, :], PB[0:128, 0:c.NVH]),
                reads=["PB"], writes=["gbtm"])
            k.emit("vector", lambda e: e.tensor_copy(
                s_btm[:, :], PB[0:128, 64:64 + c.NVH]),
                reads=["PB"], writes=["gbtm"])
            k.emit("tensor", lambda e: e.matmul(
                PB[:, 128:128 + c.NVH], s_ltri[:, :], s_gtm[:, :],
                start=True, stop=True),
                reads=["gbtm", "consts"], writes=["PB"])
            k.emit("vector", lambda e: e.tensor_copy(
                s_G[:, :], PB[:, 128:128 + c.NVH]),
                reads=["PB"], writes=["G"])
            k.emit("tensor", lambda e: e.matmul(
                PB[0:c.NVH, 256:384], s_gtm[:, :], s_ltri[:, :],
                start=True, stop=True),
                reads=["gbtm", "consts"], writes=["PB"])
            k.emit("vector", lambda e: e.tensor_copy(
                s_Gfm[:, :], PB[0:c.NVH, 256:384]),
                reads=["PB"], writes=["Gfm"])
            k.emit("tensor", lambda e: e.matmul(
                PB[:, 384:384 + c.NVH], s_onesq[:, :],
                s_gtm[:, :], start=True, stop=True),
                reads=["gbtm", "consts"], writes=["PB"])
            k.emit("vector", lambda e: e.tensor_tensor(
                s_gam[:, 0:c.NVH], PB[:, 384:384 + c.NVH], s_G[:, :],
                ALU.subtract), reads=["G", "PB"], writes=["gam"])
            k.emit("scalar", lambda e: e.activation(
                s_gam[:, 0:c.NVH], s_gam[:, 0:c.NVH], AF.Exp),
                reads=["gam"], writes=["gam"])
            k.emit("scalar", lambda e: e.activation(
                s_gam[:, c.NVH:2 * c.NVH], PB[:, 384:384 + c.NVH], AF.Exp),
                reads=["PB"], writes=["gam"])
            k.emit("scalar", lambda e: e.activation(
                s_gam[:, 2 * c.NVH:3 * c.NVH], s_G[:, :], AF.Exp),
                reads=["G"], writes=["gam"])
            k.emit("vector", lambda e: e.tensor_tensor(
                s_gam[:, 3 * c.NVH:4 * c.NVH], s_gam[:, 2 * c.NVH:3 * c.NVH],
                s_btm[:, :], ALU.mult),
                reads=["gam", "gbtm"], writes=["gam"])
            k.emit("vector", lambda e: e.tensor_scalar(
                s_gam[:, 3 * c.NVH:4 * c.NVH], s_gam[:, 3 * c.NVH:4 * c.NVH],
                -1.0, None, ALU.mult), reads=["gam"], writes=["gam"])

            if p2lvl < 14:
                continue
            for kh in range(c.NKH):
                qoc, koc = kh, c.NKH + kh
                qsl = s_qkvs[:, qoc * c.TILE + tok0:qoc * c.TILE + tok0 + 128]
                ksl = s_qkvs[:, koc * c.TILE + tok0:koc * c.TILE + tok0 + 128]
                k.emit("scalar", (lambda e, ksl=ksl: e.activation(
                    s_ksq[:, 0:128], ksl, AF.Square)),
                    reads=[f"qkvs{koc}"], writes=["ksq"])
                k.emit("scalar", (lambda e, qsl=qsl: e.activation(
                    s_ksq[:, 128:256], qsl, AF.Square)),
                    reads=[f"qkvs{qoc}"], writes=["ksq"])
                k.emit("tensor", [
                    lambda e: e.matmul(PB[0:1, 0:128], s_onesc[:, :],
                                       s_ksq[:, 0:128], start=True, stop=True),
                    lambda e: e.matmul(PB[0:1, 128:256], s_onesc[:, :],
                                       s_ksq[:, 128:256], start=True, stop=True)],
                    reads=["ksq", "consts"], writes=["PB"])
                k.emit("scalar", lambda e: e.activation(
                    s_sks[:, 0:128], PB[0:1, 0:128], AF.Sqrt,
                    bias=cst(1e-6, 1)),
                    reads=["PB", "consts"], writes=["sks"])
                k.emit("scalar", lambda e: e.activation(
                    s_sks[:, 128:256], PB[0:1, 128:256], AF.Sqrt,
                    scale=cst(128.0, 1), bias=cst(128e-6, 1)),
                    reads=["PB", "consts"], writes=["sks"])
                k.emit("vector", lambda e: e.reciprocal(
                    s_sks[:, :], s_sks[:, :]), reads=["sks"], writes=["sks"])
                k.emit("tensor", [
                    lambda e: e.matmul(PB[:, 0:128], s_ones1[:, :],
                                       s_sks[:, 0:128], start=True, stop=True),
                    lambda e: e.matmul(PB[:, 128:256], s_ones1[:, :],
                                       s_sks[:, 128:256], start=True, stop=True)],
                    reads=["sks", "consts"], writes=["PB"])
                k.emit("vector", (lambda e, ksl=ksl: e.tensor_tensor(
                    s_ktf[:, :], ksl, PB[:, 0:128], ALU.mult)),
                    reads=[f"qkvs{koc}", "PB"], writes=["ktf"])
                k.emit("vector", (lambda e, qsl=qsl: e.tensor_tensor(
                    s_qtf[:, :], qsl, PB[:, 128:256], ALU.mult)),
                    reads=[f"qkvs{qoc}", "PB"], writes=["qtf"])
                k.emit("tensor", lambda e: e.transpose(
                    PF0[:, 0:128], s_ktf[:, :], s_eyebf[:, :]),
                    reads=["ktf", "consts"], writes=["PF0"])
                k.emit("vector", lambda e: e.tensor_copy(
                    s_ktm[:, :], PF0[:, 0:128]), reads=["PF0"], writes=["ktm"])
                k.emit("tensor", lambda e: e.matmul(
                    PA[:, 0:128], s_ktf[:, :], s_ktf[:, :],
                    start=True, stop=True), reads=["ktf"], writes=["PA"])
                k.emit("tensor", lambda e: e.matmul(
                    PA[:, 128:256], s_ktf[:, :], s_qtf[:, :],
                    start=True, stop=True), reads=["ktf", "qtf"], writes=["PA"])

                if p2lvl < 15:
                    continue
                for hh in (2 * kh, 2 * kh + 1):
                    voc = 2 * c.NKH + hh
                    k.emit("tensor", (lambda e, hh=hh: e.matmul(
                        PB[:, 256:384], s_rsel[:, 128 * hh:128 * (hh + 1)],
                        s_Gfm[:, :], start=True, stop=True)),
                        reads=["Gfm", "consts"], writes=["PB"])
                    k.emit("vector", (lambda e, hh=hh: e.scalar_tensor_tensor(
                        s_argE[:, :], PB[:, 256:384], s_G[:, hh:hh + 1],
                        s_maskE[:, :], ALU.subtract, ALU.add)),
                        reads=["PB", "G", "consts"], writes=["argE"])
                    k.emit("vector", (lambda e, hh=hh: e.scalar_tensor_tensor(
                        s_argN[:, :], PB[:, 256:384], s_G[:, hh:hh + 1],
                        s_mask2[:, :], ALU.subtract, ALU.add)),
                        reads=["PB", "G", "consts"], writes=["argN"])
                    k.emit("scalar", lambda e: e.activation(
                        s_E[:, :], s_argE[:, :], AF.Exp),
                        reads=["argE"], writes=["E"])
                    k.emit("scalar", lambda e: e.activation(
                        s_E2[:, :], s_argN[:, :], AF.Exp, scale=cst(-1.0)),
                        reads=["argN", "consts"], writes=["E2"])
                    if p2lvl < 1520:
                        continue
                    k.emit("vector", lambda e: e.tensor_tensor(
                        s_A2T[:, :], s_E[:, :], PA[:, 128:256], ALU.mult),
                        reads=["E", "PA"], writes=["A2T"])
                    k.emit("vector", lambda e: e.tensor_tensor(
                        s_es[:, :], s_E[:, :], s_mask01[:, :], ALU.mult),
                        reads=["E", "consts"], writes=["es"])
                    k.emit("vector", lambda e: e.tensor_tensor(
                        s_ntm[:, :], s_es[:, :], PA[:, 0:128], ALU.mult),
                        reads=["es", "PA"], writes=["ntm"])
                    k.emit("tensor", (lambda e, hh=hh, tok0=tok0: e.matmul(
                        PB[:, 256:384], s_rsel[:, 128 * hh:128 * (hh + 1)],
                        s_nbf[:, tok0:tok0 + 128],
                        start=True, stop=True)),
                        reads=["nbf", "consts"], writes=["PB"])
                    k.emit("vector", lambda e: e.tensor_tensor(
                        s_NT[:, :], s_ntm[:, :], PB[:, 256:384], ALU.mult),
                        reads=["ntm", "PB"], writes=["NT"])
                    k.emit("vector", lambda e: e.tensor_tensor(
                        s_es[:, :], s_E2[:, :], PA[:, 0:128], ALU.mult),
                        reads=["E2", "PA"], writes=["es"])
                    k.emit("vector", (lambda e, hh=hh: e.tensor_scalar(
                        s_N[:, :], s_es[:, :],
                        s_btm[:, hh:hh + 1], -1.0,
                        ALU.mult, ALU.mult)),
                        reads=["es", "gbtm"], writes=["N"])
                    if p2lvl < 1530:
                        continue
                    k.emit("vector", lambda e: e.tensor_tensor(
                        s_TT[0][:, :], s_NT[:, :], s_eyebf[:, :], ALU.add),
                        reads=["NT", "consts"], writes=["TT0"])
                    k.emit("vector", lambda e: e.tensor_copy(
                        s_P[0][:, :], s_N[:, :]), reads=["N"], writes=["P0"])
                    k.emit("vector", lambda e: e.tensor_copy(
                        s_PT[0][:, :], s_NT[:, :]), reads=["NT"], writes=["PT0"])
                    pp = 0
                    for lv in range(1, c.NLEV + 1 if p2lvl >= 1532 else 1):
                        np_, op_ = 1 - pp, pp
                        last = (lv == c.NLEV)
                        k.emit("tensor", (lambda e, op_=op_: e.matmul(
                            PC[:, 0:128], s_PT[op_][:, :], s_P[op_][:, :],
                            start=True, stop=True)),
                            reads=[f"P{op_}", f"PT{op_}"], writes=["PC"])
                        if not last and p2lvl >= 1533:
                            k.emit("tensor", (lambda e, op_=op_: e.matmul(
                                PC[:, 128:256], s_P[op_][:, :], s_PT[op_][:, :],
                                start=True, stop=True)),
                                reads=[f"P{op_}", f"PT{op_}"], writes=["PC"])
                        k.emit("scalar", (lambda e, np_=np_: e.activation(
                            s_P[np_][:, :], PC[:, 0:128], AF.Copy)),
                            reads=["PC"], writes=[f"P{np_}"])
                        if p2lvl < 1533:
                            pp = np_
                            continue
                        if not last:
                            k.emit("vector", (lambda e, np_=np_: e.tensor_copy(
                                s_PT[np_][:, :], PC[:, 128:256])),
                                reads=["PC"], writes=[f"PT{np_}"])
                        if p2lvl < 1534:
                            pp = np_
                            continue
                        k.emit("tensor", (lambda e, np_=np_, op_=op_: e.matmul(
                            PC[:, 256:384], s_P[np_][:, :], s_TT[op_][:, :],
                            start=True, stop=True)),
                            reads=[f"P{np_}", f"TT{op_}"], writes=["PC"])
                        if p2lvl < 1535:
                            pp = np_
                            continue
                        k.emit("vector", (lambda e, np_=np_, op_=op_:
                            e.tensor_tensor(
                                s_TT[np_][:, :], PC[:, 256:384],
                                s_TT[op_][:, :], ALU.add)),
                            reads=["PC", f"TT{op_}"], writes=[f"TT{np_}"])
                        pp = np_
                    if p2lvl < 1540:
                        continue
                    k.emit("tensor", (lambda e, hh=hh: e.matmul(
                        PD[:, 0:128], s_ktf[:, :], s_Sbf[hh][:, :],
                        start=True, stop=True)),
                        reads=["ktf", f"Sbf{hh}"], writes=["PD"])
                    k.emit("tensor", (lambda e, voc=voc, tok0=tok0: e.transpose(
                        PF0[:, 128:256],
                        s_qkvs[:, voc * c.TILE + tok0:voc * c.TILE + tok0 + 128],
                        s_eyebf[:, :])),
                        reads=[f"qkvs{voc}", "consts"], writes=["PF0"])
                    k.emit("scalar", (lambda e, hh=hh: e.activation(
                        s_dump[:, :], PF0[:, 128:256], AF.Copy,
                        scale=s_btm[:, hh:hh + 1])),
                        reads=["PF0", "gbtm"], writes=["dump"])
                    k.emit("vector", (lambda e, hh=hh: e.scalar_tensor_tensor(
                        s_RHS[:, :], PD[:, 0:128],
                        s_gam[:, 3 * c.NVH + hh:3 * c.NVH + hh + 1],
                        s_dump[:, :], ALU.mult, ALU.add)),
                        reads=["PD", "gam", "dump"], writes=["RHS"])
                    if p2lvl < 1550:
                        continue
                    k.emit("tensor", (lambda e, pp=pp: e.matmul(
                        PC[:, 384:512], s_TT[pp][:, :], s_RHS[:, :],
                        start=True, stop=True)),
                        reads=[f"TT{pp}", "RHS"], writes=["PC"])
                    k.emit("scalar", lambda e: e.activation(
                        s_Xbf[:, :], PC[:, 384:512], AF.Copy),
                        reads=["PC"], writes=["Xbf"])
                    k.emit("vector", (lambda e, hh=hh: e.tensor_scalar(
                        s_Xw[:, :], PC[:, 384:512], s_gam[:, hh:hh + 1],
                        None, ALU.mult)),
                        reads=["PC", "gam"], writes=["Xw"])
                    if p2lvl < 1560:
                        continue
                    k.emit("tensor", (lambda e, hh=hh: e.matmul(
                        PD[:, 128:256], s_qtf[:, :], s_Sbf[hh][:, :],
                        start=True, stop=True)),
                        reads=["qtf", f"Sbf{hh}"], writes=["PD"])
                    k.emit("tensor", lambda e: e.matmul(
                        PD[:, 256:384], s_A2T[:, :], s_Xbf[:, :],
                        start=True, stop=True),
                        reads=["A2T", "Xbf"], writes=["PD"])
                    k.emit("scalar", (lambda e, hh=hh: e.activation(
                        s_es[:, :], PD[:, 128:256], AF.Copy,
                        scale=s_gam[:, 2 * c.NVH + hh:2 * c.NVH + hh + 1])),
                        reads=["PD", "gam"], writes=["es"])
                    k.emit("vector", lambda e: e.tensor_tensor(
                        s_Otm[:, :], s_es[:, :], PD[:, 256:384], ALU.add),
                        reads=["es", "PD"], writes=["Otm"])
                    if p2lvl < 1570:
                        continue
                    k.emit("tensor", lambda e: e.matmul(
                        PD[:, 384:512], s_ktm[:, :], s_Xw[:, :],
                        start=True, stop=True),
                        reads=["ktm", "Xw"], writes=["PD"])
                    k.emit("vector", (lambda e, hh=hh: e.scalar_tensor_tensor(
                        s_S[hh][:, :], s_S[hh][:, :],
                        s_gam[:, c.NVH + hh:c.NVH + hh + 1],
                        PD[:, 384:512], ALU.mult, ALU.add)),
                        reads=[f"S{hh}", "gam", "PD"], writes=[f"S{hh}"])
                    k.emit("scalar", (lambda e, hh=hh: e.activation(
                        s_Sbf[hh][:, :], s_S[hh][:, :], AF.Copy)),
                        reads=[f"S{hh}"], writes=[f"Sbf{hh}"])
                    if p2lvl < 1580:
                        continue
                    k.emit("scalar", lambda e: e.activation(
                        s_dump[:, :], s_Otm[:, :], AF.Square,
                        accum_out=s_oc2[:, 0:1]),
                        reads=["Otm"], writes=["dump", "oc2"])
                    k.emit("scalar", lambda e: e.activation(
                        s_oc2[:, 1:2], s_oc2[:, 0:1], AF.Sqrt,
                        scale=cst(1.0 / 128), bias=cst(1e-6)),
                        reads=["oc2", "consts"], writes=["oc2"])
                    k.emit("vector", lambda e: e.reciprocal(
                        s_oc2[:, 1:2], s_oc2[:, 1:2]),
                        reads=["oc2"], writes=["oc2"])
                    k.emit("vector", (lambda e, hh=hh: e.scalar_tensor_tensor(
                        s_Xbf[:, :], s_Otm[:, :], s_oc2[:, 1:2],
                        s_zg[:, 128 * hh:128 * (hh + 1)],
                        ALU.mult, ALU.mult)),
                        reads=["Otm", "oc2", "zg"], writes=["Xbf"])
                    k.emit("tensor", lambda e: e.transpose(
                        PF1[:, 0:128], s_Xbf[:, :], s_eyebf[:, :]),
                        reads=["Xbf", "consts"], writes=["PF1"])
                    k.emit("vector", (lambda e, hh=hh, tok0=tok0: e.tensor_copy(
                        s_ofm[hh][:, tok0:tok0 + 128], PF1[:, 0:128])),
                        reads=["PF1"], writes=["ofm"])
        if p2lvl < 16:
            continue
        blk = t // (c.NT // 4)
        colo = (t % (c.NT // 4)) * c.TILE
        for outc in range(c.OUTC):
            par = outc % 2
            bank, bkey = (PC, "PC") if par == 0 else (PD, "PD")
            k.emit("tensor", [
                (lambda e, v=v, outc=outc, bank=bank: e.matmul(
                    bank[:, 0:c.TILE],
                    s_wout[v][:, 128 * outc:128 * (outc + 1)],
                    s_ofm[v][:, :],
                    start=(v == 0), stop=(v == c.NVH - 1)))
                for v in range(c.NVH)],
                reads=["woutsb", "ofm"], writes=[bkey])
            k.emit("scalar", (lambda e, bank=bank, par=par: e.activation(
                s_sfm[par][:, 0:c.TILE], bank[:, 0:c.TILE], AF.Copy)),
                reads=[bkey], writes=[f"sfm{par}"])
            k.dma("sync", f"d_ri{par}", (lambda e, outc=outc, blk=blk,
                                         colo=colo, par=par: e.dma_start(
                out=rsin[blk, 128 * outc:128 * (outc + 1),
                         colo:colo + c.TILE],
                in_=s_sfm[par][:, 0:c.TILE])),
                reads=[f"sfm{par}"], writes=["rsin"])

    if upto <= 2:
        k.fence()
        stP2.close()
        return finish_dummy()

    k.emit("gpsimd", (lambda e: e.collective_compute(
        "ReduceScatter", ALU.add,
        replica_groups=[[0, 1, 2, 3], [4, 5, 6, 7]],
        ins=[rsin.ap().opt()], outs=[rsout.ap().opt()])),
        reads=["rsin"], writes=["rsout"])

    k.fence()
    stP2.close()
    if upto <= 3:
        return finish_dummy()

    st2 = ExitStack()
    s_rs = st2.enter_context(nc.sbuf_tensor("s_rs", [1, c.TQ], BF16))
    s_gus = st2.enter_context(nc.sbuf_tensor("s_gus", [1, c.TQ], BF16))

    st3 = ExitStack()
    s_gts = st3.enter_context(nc.sbuf_tensor("s_gts", [128, c.TQ], F32))
    s_gud = [st3.enter_context(nc.sbuf_tensor(f"s_gud{i}", [128, c.W4], BF16))
             for i in range(2)]
    s_xnq = [st3.enter_context(nc.sbuf_tensor(f"s_xnq{f}", [128, c.TQ], BF16))
             for f in range(c.NF)]
    s_wgu = [[[st3.enter_context(
        nc.sbuf_tensor(f"s_wgu{p}_{m}_{f}", [128, 512], BF16))
        for f in range(c.NF)] for m in range(2)] for p in range(2)]

    k.dma("sync", "d_xnq", [
        (lambda e, f=f: e.dma_start(
            out=s_xnq[f][:, :],
            in_=xnq_sc[:, 128 * f:128 * (f + 1)], transpose=True))
        for f in range(c.NF)],
        reads=["xnq_sc"], writes=["xnq"])
    TQH = c.TQ // c.W4
    GUPB = [(PP0, "PP0"), (PP1, "PP1"), (PA, "PA"), (PB, "PB")]
    for h2 in range(TQH):
        bank, bkey = (PC, "PC") if h2 % 2 == 0 else (PD, "PD")
        k.emit("tensor", [
            (lambda e, f=f, h2=h2, bank=bank: e.matmul(
                bank[0:1, 0:c.W4], s_onesb[:, :],
                s_xnq[f][:, c.W4 * h2:c.W4 * (h2 + 1)],
                start=(f == 0), stop=(f == c.NF - 1)))
            for f in range(c.NF)],
            reads=["xnq", "consts"], writes=[bkey])
        k.emit("vector", (lambda e, h2=h2, bank=bank: e.tensor_copy(
            s_rs[:, c.W4 * h2:c.W4 * (h2 + 1)], bank[0:1, 0:c.W4])),
            reads=[bkey], writes=["rs"])

    def load_wgu(hb, par):
        specs = []
        for m, wsrc in ((0, wgT), (1, wuT)):
            for f in range(c.NF):
                specs.append(lambda e, m=m, f=f, hb=hb, par=par, wsrc=wsrc:
                             e.dma_start(
                                 out=s_wgu[par][m][f][:, :],
                                 in_=wsrc[128 * f:128 * (f + 1),
                                          512 * hb:512 * (hb + 1)]))
        k.dma("gpsimd", f"d_f{par}", specs, writes=[f"wgu{par}"])

    load_wgu(0, 0)
    for hb in range(c.NHB):
        par = hb % 2
        if hb + 1 < c.NHB:
            load_wgu(hb + 1, 1 - par)
        for oc in range(4 if c.HID >= 512 else c.NHC):
            hc = 4 * hb + oc
            for h2 in range(TQH):
                gb, gk = GUPB[2 * (h2 % 2)]
                ub, uk = GUPB[2 * (h2 % 2) + 1]
                k.emit("tensor", [
                    (lambda e, f=f, oc=oc, h2=h2, par=par, gb=gb: e.matmul(
                        gb[:, 0:c.W4],
                        s_wgu[par][0][f][:, 128 * oc:128 * (oc + 1)],
                        s_xnq[f][:, c.W4 * h2:c.W4 * (h2 + 1)],
                        start=(f == 0), stop=False))
                    for f in range(c.NF)] + [
                    (lambda e, h2=h2, gb=gb: e.matmul(
                        gb[:, 0:c.W4], s_m128[:, :],
                        s_rs[:, c.W4 * h2:c.W4 * (h2 + 1)],
                        start=False, stop=True))],
                    reads=[f"wgu{par}", "xnq", "rs", "consts"], writes=[gk])
                k.emit("tensor", [
                    (lambda e, f=f, oc=oc, h2=h2, par=par, ub=ub: e.matmul(
                        ub[:, 0:c.W4],
                        s_wgu[par][1][f][:, 128 * oc:128 * (oc + 1)],
                        s_xnq[f][:, c.W4 * h2:c.W4 * (h2 + 1)],
                        start=(f == 0), stop=False))
                    for f in range(c.NF)] + [
                    (lambda e, h2=h2, ub=ub: e.matmul(
                        ub[:, 0:c.W4], s_m128[:, :],
                        s_rs[:, c.W4 * h2:c.W4 * (h2 + 1)],
                        start=False, stop=True))],
                    reads=[f"wgu{par}", "xnq", "rs", "consts"], writes=[uk])
                k.emit("scalar", (lambda e, hc=hc, h2=h2, gb=gb: e.activation(
                    s_gts[:, c.W4 * h2:c.W4 * (h2 + 1)], gb[:, 0:c.W4],
                    AF.Silu, scale=s_sg2[:, hc:hc + 1])),
                    reads=[gk, "consts"], writes=["gts"])
                gpar = (hc * TQH + h2) % 2
                k.emit("vector", (lambda e, hc=hc, h2=h2, ub=ub, gpar=gpar:
                    e.scalar_tensor_tensor(
                        s_gud[gpar][:, :],
                        ub[:, 0:c.W4], s_su2[:, hc:hc + 1],
                        s_gts[:, c.W4 * h2:c.W4 * (h2 + 1)],
                        ALU.mult, ALU.mult)),
                    reads=[uk, "consts", "gts"], writes=[f"gud{gpar}"])
                gusb, gusk = (PC, "PC") if h2 % 2 == 0 else (PD, "PD")
                k.emit("tensor", (lambda e, gpar=gpar, hc=hc, gusb=gusb:
                    e.matmul(
                        gusb[0:1, 0:c.W4], s_onesb[:, :], s_gud[gpar][:, :],
                        start=(hc == 0), stop=(hc == c.NHC - 1))),
                    reads=[f"gud{gpar}", "consts"], writes=[gusk])
                k.dma("sync", f"d_gw{gpar}", (lambda e, gpar=gpar, hc=hc,
                                              h2=h2: e.dma_start(
                    out=gu_sc[128 * hc:128 * (hc + 1),
                              c.W4 * h2:c.W4 * (h2 + 1)],
                    in_=s_gud[gpar][:, :])),
                    reads=[f"gud{gpar}"], writes=["gu_sc"])
    for h2 in range(TQH):
        bank, bkey = (PC, "PC") if h2 % 2 == 0 else (PD, "PD")
        k.emit("vector", (lambda e, h2=h2, bank=bank: e.tensor_copy(
            s_gus[:, c.W4 * h2:c.W4 * (h2 + 1)], bank[0:1, 0:c.W4])),
            reads=[bkey], writes=["gus"])
    k.fence()
    st3.close()

    if upto <= 4:
        return finish_dummy()
    st4 = ExitStack()
    s_acc = st4.enter_context(
        nc.sbuf_tensor("s_acc", [128, c.OUTC * c.TQ], F32))
    s_wd = [[st4.enter_context(
        nc.sbuf_tensor(f"s_wd{p}_{j}", [128, c.D], BF16))
        for j in range(c.GSZ)] for p in range(2)]
    s_gur = [[st4.enter_context(
        nc.sbuf_tensor(f"s_gur{p}_{j}", [128, c.TQ], BF16))
        for j in range(c.GSZ)] for p in range(2)]
    s_fin = [st4.enter_context(
        nc.sbuf_tensor(f"s_fin{i}", [128, c.TQ], F32)) for i in range(4)]

    def load_wd(g, par):
        hcs = list(range(c.GSZ * g, min(c.GSZ * (g + 1), c.NHC)))
        k.dma("gpsimd", f"d_wd{par}", [
            (lambda e, j=j, hc=hc, par=par: e.dma_start(
                out=s_wd[par][j][:, :], in_=wdT[128 * hc:128 * (hc + 1), :]))
            for j, hc in enumerate(hcs)], writes=[f"wd{par}"])
        k.dma("sync", f"d_gr{par}", [
            (lambda e, j=j, hc=hc, par=par: e.dma_start(
                out=s_gur[par][j][:, :],
                in_=gu_sc[128 * hc:128 * (hc + 1), :]))
            for j, hc in enumerate(hcs)],
            reads=["gu_sc"], writes=[f"gur{par}"])

    load_wd(0, 0)
    for g in range(c.NGRP):
        par = g % 2
        if g + 1 < c.NGRP:
            load_wd(g + 1, 1 - par)
        hcs = list(range(c.GSZ * g, min(c.GSZ * (g + 1), c.NHC)))
        for outc in range(c.OUTC):
            for h2 in range(TQH):
                bank, bkey = GUPB[2 * (outc % 2) + h2 % 2]
                fns = [
                    (lambda e, j=j, hc=hc, outc=outc, h2=h2, par=par,
                     bank=bank, hn=len(hcs), g=g: e.matmul(
                        bank[:, 0:c.W4],
                        s_wd[par][j][:, 128 * outc:128 * (outc + 1)],
                        s_gur[par][j][:, c.W4 * h2:c.W4 * (h2 + 1)],
                        start=(j == 0),
                        stop=(j == hn - 1 and g != c.NGRP - 1)))
                    for j, hc in enumerate(hcs)]
                if g == c.NGRP - 1:
                    fns.append(lambda e, h2=h2, bank=bank: e.matmul(
                        bank[:, 0:c.W4], s_m128[:, :],
                        s_gus[:, c.W4 * h2:c.W4 * (h2 + 1)],
                        start=False, stop=True))
                k.emit("tensor", fns,
                       reads=[f"wd{par}", f"gur{par}", "gus", "consts"],
                       writes=[bkey])
                ao = outc * c.TQ + c.W4 * h2
                if g == 0:
                    k.emit("vector", (lambda e, ao=ao, bank=bank: e.tensor_copy(
                        s_acc[:, ao:ao + c.W4], bank[:, 0:c.W4])),
                        reads=[bkey], writes=[f"acc{outc}"])
                else:
                    k.emit("vector", (lambda e, ao=ao, bank=bank, outc=outc:
                        e.tensor_tensor(
                            s_acc[:, ao:ao + c.W4], bank[:, 0:c.W4],
                            s_acc[:, ao:ao + c.W4], ALU.add)),
                        reads=[bkey, f"acc{outc}"], writes=[f"acc{outc}"])

    for outc in range(c.OUTC):
        k.dma("sync", "d_ro", (lambda e, outc=outc: e.dma_start(
            out=s_fin[0][:, :], in_=rsout[128 * outc:128 * (outc + 1), :])),
            reads=["rsout"], writes=["fin0"])
        k.dma("sync", "d_xt", (lambda e, outc=outc: e.dma_start(
            out=s_fin[1][:, :], in_=xqT[128 * outc:128 * (outc + 1), :])),
            writes=["fin1"])
        k.emit("scalar", (lambda e, outc=outc: e.activation(
            s_fin[2][:, :], s_acc[:, outc * c.TQ:(outc + 1) * c.TQ], AF.Silu,
            scale=s_sd2[:, outc:outc + 1])),
            reads=[f"acc{outc}", "consts"], writes=["fin2"])
        k.emit("vector", lambda e: e.tensor_tensor(
            s_fin[3][:, :], s_fin[0][:, :], s_fin[2][:, :], ALU.mult),
            reads=["fin0", "fin2"], writes=["fin3"])
        k.emit("vector", lambda e: e.tensor_tensor(
            s_fin[3][:, :], s_fin[3][:, :], s_fin[1][:, :], ALU.add),
            reads=["fin3", "fin1"], writes=["fin3"])
        k.dma("sync", "d_out", (lambda e, outc=outc: e.dma_start(
            out=outfm[128 * outc:128 * (outc + 1), :], in_=s_fin[3][:, :])),
            reads=["fin3"], writes=[f"out{outc}"])

    k._wait("gpsimd", "d_out", k.count["d_out"])

    semh = {}
    for name in sorted(set(list(k.count.keys()) + list(K.ESEM.values()))):
        semh[name] = st.enter_context(nc.semaphore(name))
    k.replay(nc, semh)
    st4.close()
    st2.close()
    st.close()
    return nc


def stage(cfg, A):
    c = cfg
    x = np.ascontiguousarray(np.asarray(A["L0_attn_hidden_states"], np.float32))
    Wqkv = np.asarray(A["L0_attn_W_qkv"], np.float32)
    Wz = np.asarray(A["L0_attn_W_z"], np.float32)
    Wb = np.asarray(A["L0_attn_W_b"], np.float32)
    Wa = np.asarray(A["L0_attn_W_a"], np.float32)
    convw = np.asarray(A["L0_attn_conv_w"], np.float32)
    dtb = np.asarray(A["L0_attn_dt_bias"], np.float32)
    alog = np.asarray(A["L0_attn_A_log"], np.float32)
    normw = np.asarray(A["L0_attn_norm_w"], np.float32)
    Wout = np.asarray(A["L0_attn_W_out"], np.float32)
    wqg = np.asarray(A["L0_ffn_wq_gate"], np.int32)
    sg = np.asarray(A["L0_ffn_s_gate"], np.float32)
    wqu = np.asarray(A["L0_ffn_wq_up"], np.int32)
    su = np.asarray(A["L0_ffn_s_up"], np.float32)
    wqd = np.asarray(A["L0_ffn_wq_down"], np.int32)
    sd = np.asarray(A["L0_ffn_s_down"], np.float32)

    KD = 4 * c.QW
    wgT = np.ascontiguousarray(wqg.T)
    wuT = np.ascontiguousarray(wqu.T)
    wdT = np.ascontiguousarray(wqd.T)
    sg2 = np.ascontiguousarray(sg.reshape(c.NHC, 128).T)
    su2 = np.ascontiguousarray(su.reshape(c.NHC, 128).T)
    sd2 = np.ascontiguousarray(sd.reshape(c.OUTC, 128).T)
    idx = np.arange(128)
    consts = dict(
        eyebf=np.eye(128, dtype=BF),
        eyef=np.eye(128, dtype=np.float32),
        ltri=np.triu(np.ones((128, 128), np.float32)),
        maskE=np.where(idx[None, :] >= idx[:, None], 0.0, -1e30).astype(np.float32),
        mask2=np.where(idx[None, :] < idx[:, None], 0.0, 1e30).astype(np.float32),
        mask01=np.triu(np.ones((128, 128), np.float32), 1),
        rowsel=np.ascontiguousarray(np.repeat(
            np.eye(cfg.NVH, dtype=np.float32), 128, axis=1)),
        normw4=np.ascontiguousarray(np.tile(normw, 512 // normw.shape[0])[None, :]).astype(np.float32),
    )
    in_maps = []
    for core in range(8):
        b, hg = core // 4, core % 4
        qsl = slice(c.QW * hg, c.QW * (hg + 1))
        ksl = slice(KD + c.QW * hg, KD + c.QW * (hg + 1))
        vsl = slice(2 * KD + c.VW * hg, 2 * KD + c.VW * (hg + 1))
        zsl = slice(c.VW * hg, c.VW * (hg + 1))
        hsl = slice(c.NVH * hg, c.NVH * (hg + 1))
        wcomb = np.concatenate([
            Wqkv[:, qsl], Wqkv[:, ksl], Wqkv[:, vsl], Wz[:, zsl],
            Wa[:, hsl], Wb[:, hsl]], axis=1)
        cw = np.concatenate([convw[qsl], convw[ksl], convw[vsl]], axis=0)
        xb = x[b]
        m = dict(
            xfull=xb,
            xq=np.ascontiguousarray(xb[c.TQ * hg:c.TQ * (hg + 1)]),
            xqT=np.ascontiguousarray(xb[c.TQ * hg:c.TQ * (hg + 1)].T),
            wcomb=np.ascontiguousarray(wcomb),
            convw=np.ascontiguousarray(cw),
            alog=np.ascontiguousarray(alog[hsl][:, None]),
            dtb=np.ascontiguousarray(dtb[hsl][:, None]),
            wout=np.ascontiguousarray(Wout[c.VW * hg:c.VW * (hg + 1)]),
            wgT=wgT, wuT=wuT, wdT=wdT, sg2=sg2, su2=su2, sd2=sd2,
            **consts)
        in_maps.append(m)
    return in_maps


def assemble(cfg, outs):
    c = cfg
    out = np.empty((2, c.T, c.D), np.float32)
    for core in range(8):
        b, hg = core // 4, core % 4
        out[b, c.TQ * hg:c.TQ * (hg + 1), :] = outs[core].T
    return out


def kernel(**inputs):
    cfg = REAL
    in_maps = stage(cfg, inputs)
    nc = build(cfg)
    res = run_bass_kernel_spmd(nc, in_maps, core_ids=list(range(8)))
    return assemble(cfg, [np.asarray(r["outfm"]) for r in res.results])


# version 06
# speedup vs baseline: 1.0454x; 1.0454x over previous
import numpy as np
import ml_dtypes
import concourse.bass as bass
import concourse.mybir as mybir
from concourse.bass_utils import run_bass_kernel_spmd
from contextlib import ExitStack

F32 = mybir.dt.float32
BF16 = mybir.dt.bfloat16
I32 = mybir.dt.int32
AF = mybir.ActivationFunctionType
ALU = mybir.AluOpType
AX = mybir.AxisListType
BF = ml_dtypes.bfloat16


class Cfg:
    def __init__(self, T, TILE, D, NKH, HID):
        self.T, self.TILE, self.D, self.NKH, self.HID = T, TILE, D, NKH, HID
        self.NVH = 2 * NKH
        self.NF = D // 128
        self.NT = T // TILE
        self.NCH = TILE // 128
        self.TQ = T // 4
        self.QW = NKH * 128
        self.VW = self.NVH * 128
        self.ABW = 2 * self.NVH
        self.WC = 2 * self.QW + self.VW
        self.WCOLS = self.WC + self.VW + self.ABW
        self.OUTC = D // 128
        self.NHC = HID // 128
        self.NHB = HID // 512
        self.GSZ = 4
        self.NGRP = (self.NHC + self.GSZ - 1) // self.GSZ
        self.NLEV = 4
        self.NQB = self.TQ // 128
        self.NTB = T // 128
        self.W2 = min(512, TILE)
        self.W4 = min(512, self.TQ)
        self.NZC = self.VW // 512 if self.VW >= 512 else 0
        self.ZW = 512 if self.VW >= 512 else self.VW
        if self.NZC == 0:
            self.NZC = 1


REAL = Cfg(T=4096, TILE=512, D=2048, NKH=4, HID=5632)
TINY = Cfg(T=512, TILE=128, D=256, NKH=2, HID=512)


class K:
    ENGS = ("tensor", "vector", "scalar", "gpsimd", "sync")
    ESEM = {"tensor": "s_pe", "vector": "s_dve", "scalar": "s_act",
            "gpsimd": "s_gp", "sync": "s_sp"}

    def __init__(self):
        self.streams = {e: [] for e in K.ENGS}
        self.count = {}
        self.mark = {e: {} for e in K.ENGS}
        self.writer = {}
        self.readers = {}
        self.dma_pend = {}
        self.sem_names = set(K.ESEM.values())

    def _wait(self, eng, sem, val):
        if val <= 0:
            return
        if self.mark[eng].get(sem, 0) >= val:
            return
        self.mark[eng][sem] = val
        self.streams[eng].append(("w", sem, val))

    def _deps(self, eng, reads, writes):
        for k in reads:
            for s, v in self.writer.get(k, {}).items():
                self._wait(eng, s, v)
        for k in writes:
            for s, v in self.writer.get(k, {}).items():
                self._wait(eng, s, v)
            for s, v in self.readers.get(k, {}).items():
                self._wait(eng, s, v)

    def _post(self, sem, v, reads, writes):
        for k in writes:
            self.writer.setdefault(k, {})[sem] = v
            self.readers[k] = {}
        for k in reads:
            self.readers.setdefault(k, {})[sem] = v

    PSUM_KEYS = frozenset(
        ["PP0", "PP1", "PA", "PB", "PC", "PD", "PF0", "PF1"])

    def _excl(self, reads, writes):
        pr = [k for k in reads if k in K.PSUM_KEYS]
        if pr:
            reads = [k for k in reads if k not in K.PSUM_KEYS]
            writes = list(writes) + pr
        return reads, writes

    def emit(self, eng, fns, reads=(), writes=()):
        if not isinstance(fns, (list, tuple)):
            fns = [fns]
        reads, writes = self._excl(reads, writes)
        self._deps(eng, reads, writes)
        sem = K.ESEM[eng]
        self.count[sem] = self.count.get(sem, 0) + 1
        v = self.count[sem]
        for i, fn in enumerate(fns):
            self.streams[eng].append(
                ("i", fn, sem if i == len(fns) - 1 else None, 1))
        self._post(sem, v, reads, writes)
        return v

    def dma(self, eng, sem, fns, reads=(), writes=()):
        if not isinstance(fns, (list, tuple)):
            fns = [fns]
        reads, writes = self._excl(reads, writes)
        self.sem_names.add(sem)
        if self.dma_pend.get(sem):
            self._wait(eng, sem, self.count.get(sem, 0))
        self._deps(eng, reads, writes)
        for fn in fns:
            self.count[sem] = self.count.get(sem, 0) + 16
            self.streams[eng].append(("i", fn, sem, 16))
        v = self.count[sem]
        self.dma_pend[sem] = True
        self._post(sem, v, reads, writes)
        return v

    def fence(self):
        for eng in K.ENGS:
            own = K.ESEM[eng]
            for sem, cnt in list(self.count.items()):
                if sem != own:
                    self._wait(eng, sem, cnt)

    def replay(self, nc, semh):
        with nc.Block() as block:
            for name in K.ENGS:
                stream = self.streams[name]
                if not stream:
                    continue

                def body(eng, stream=stream):
                    for it in stream:
                        if it[0] == "w":
                            eng.wait_ge(semh[it[1]], it[2])
                        else:
                            bi = it[1](eng)
                            if it[2] is not None:
                                bi.then_inc(semh[it[2]], it[3])
                getattr(block, name)(body)


def build(cfg, upto=5, p2lvl=10**9):
    c = cfg
    nc = bass.Bass("TRN2", target_bir_lowering=False, debug=False)
    k = K()

    dp = nc.declare_dram_parameter
    xfull = dp("xfull", [c.T, c.D], F32, isOutput=False)
    xq = dp("xq", [c.TQ, c.D], F32, isOutput=False)
    xqT = dp("xqT", [c.D, c.TQ], F32, isOutput=False)
    wgc = dp("wgc", [c.NF, 128, c.WC], F32, isOutput=False)
    wzc = dp("wzc", [c.NF, c.NZC, 128, c.ZW], F32, isOutput=False)
    wabc = dp("wabc", [c.NF, 128, c.ABW], F32, isOutput=False)
    convw = dp("convw", [c.WC, 4], F32, isOutput=False)
    alog = dp("alog", [c.NVH, 1], F32, isOutput=False)
    dtb = dp("dtb", [c.NVH, 1], F32, isOutput=False)
    normw4 = dp("normw4", [1, 512], F32, isOutput=False)
    wout = dp("wout", [c.VW, c.D], F32, isOutput=False)
    wgT = dp("wgT", [c.D, c.HID], I32, isOutput=False)
    wuT = dp("wuT", [c.D, c.HID], I32, isOutput=False)
    wdT = dp("wdT", [c.HID, c.D], I32, isOutput=False)
    sg2 = dp("sg2", [128, c.NHC], F32, isOutput=False)
    su2 = dp("su2", [128, c.NHC], F32, isOutput=False)
    sd2 = dp("sd2", [128, c.OUTC], F32, isOutput=False)
    eyebf = dp("eyebf", [128, 128], BF16, isOutput=False)
    eyef = dp("eyef", [128, 128], F32, isOutput=False)
    ltri = dp("ltri", [128, 128], F32, isOutput=False)
    maskE = dp("maskE", [128, 128], F32, isOutput=False)
    mask2 = dp("mask2", [128, 128], F32, isOutput=False)
    mask01 = dp("mask01", [128, 128], F32, isOutput=False)
    rowsel = dp("rowsel", [c.NVH, 128 * c.NVH], F32, isOutput=False)
    outfm = dp("outfm", [c.D, c.TQ], F32, isOutput=True)

    xn_sc = nc.dram_tensor("xn_sc", [c.T, c.D], BF16)
    xnq_sc = nc.dram_tensor("xnq_sc", [c.TQ, c.D], BF16)
    ofm_sc = nc.dram_tensor("ofm_sc", [c.NVH, 128, c.T], BF16)
    rsin = nc.dram_tensor("rsin", [4, c.D, c.TQ], F32)
    gu_sc = nc.dram_tensor("gu_sc", [c.NHC * 128, c.TQ], BF16)
    rsout = nc.dram_tensor("rsout", [c.D, c.TQ], F32)

    st = ExitStack()
    sb = lambda n, s, d: st.enter_context(nc.sbuf_tensor(n, s, d))
    ps = lambda n, s, d: st.enter_context(nc.psum_tensor(n, s, d))

    PP0 = ps("PP0", [128, 512], F32)
    PP1 = ps("PP1", [128, 512], F32)
    PA = ps("PA", [128, 512], F32)
    PB = ps("PB", [128, 512], F32)
    PC = ps("PC", [128, 512], F32)
    PD = ps("PD", [128, 512], F32)
    PF0 = ps("PF0", [128, 1024], BF16)
    PF1 = ps("PF1", [128, 1024], BF16)

    s_eyebf = sb("s_eyebf", [128, 128], BF16)
    s_eyef = sb("s_eyef", [128, 128], F32)
    s_ltri = sb("s_ltri", [128, 128], F32)
    s_maskE = sb("s_maskE", [128, 128], F32)
    s_mask2 = sb("s_mask2", [128, 128], F32)
    s_mask01 = sb("s_mask01", [128, 128], F32)
    s_normw = sb("s_normw", [1, 512], F32)
    s_alog = sb("s_alog", [c.NVH, 1], F32)
    s_dtb = sb("s_dtb", [c.NVH, 1], F32)
    s_nega = sb("s_nega", [c.NVH, 1], F32)
    s_cst = sb("s_cst", [128, 8], F32)
    s_ones1 = sb("s_ones1", [1, 128], F32)
    s_onesc = sb("s_onesc", [128, 1], F32)
    s_onesb = sb("s_onesb", [128, 1], BF16)
    s_m128 = sb("s_m128", [1, 128], BF16)
    s_normb = sb("s_normb", [128, 512], F32)
    s_onesq = sb("s_onesq", [128, 128], F32)
    s_rsel = sb("s_rsel", [c.NVH, 128 * c.NVH], F32)
    s_convw = sb("s_convw", [128, 4 * (c.WC // 128)], F32)
    s_sg2 = sb("s_sg2", [128, c.NHC], F32)
    s_su2 = sb("s_su2", [128, c.NHC], F32)
    s_sd2 = sb("s_sd2", [128, c.OUTC], F32)

    CST = {1e-6: 0, 128.0: 1, 128e-6: 2, 1.0 / c.D: 3, 1e-5: 4,
           1.0 / 128: 5, 1.0: 6, -1.0: 7}

    def cst(v, p=128):
        return s_cst[0:p, CST[v]:CST[v] + 1]

    k.dma("sync", "d_c", [
        (lambda e, o=o_, i=i_: e.dma_start(out=o, in_=i))
        for o_, i_ in [
            (s_eyebf[:, :], eyebf[:, :]), (s_eyef[:, :], eyef[:, :]),
            (s_ltri[:, :], ltri[:, :]), (s_maskE[:, :], maskE[:, :]),
            (s_mask2[:, :], mask2[:, :]), (s_mask01[:, :], mask01[:, :]),
            (s_normw[:, :], normw4[:, :]), (s_alog[:, :], alog[:, :]),
            (s_dtb[:, :], dtb[:, :]), (s_sg2[:, :], sg2[:, :]),
            (s_su2[:, :], su2[:, :]), (s_sd2[:, :], sd2[:, :]),
            (s_rsel[:, :], rowsel[:, :]),
        ] + [(s_convw[:, 4 * f:4 * f + 4], convw[128 * f:128 * (f + 1), :])
             for f in range(c.WC // 128)]
    ], writes=["consts"])
    for val, col in CST.items():
        k.emit("vector", (lambda e, v=val, cl=col: e.memset(
            s_cst[:, cl:cl + 1], v)), writes=["consts"])
    k.emit("vector", lambda e: e.memset(s_ones1[:, :], 1.0), writes=["consts"])
    k.emit("vector", lambda e: e.memset(s_onesc[:, :], 1.0), writes=["consts"])
    k.emit("vector", lambda e: e.memset(s_onesb[:, :], 1.0), writes=["consts"])
    k.emit("vector", lambda e: e.memset(s_m128[:, :], -128.0), writes=["consts"])
    k.emit("vector", lambda e: e.memset(s_onesq[:, :], 1.0), writes=["consts"])
    k.emit("scalar", lambda e: e.activation(s_nega[:, :], s_alog[:, :], AF.Exp),
           reads=["consts"], writes=["nega"])
    k.emit("vector", lambda e: e.tensor_scalar(
        s_nega[:, :], s_nega[:, :], -1.0, None, ALU.mult),
        reads=["nega"], writes=["nega"])
    k.emit("tensor", lambda e: e.matmul(
        PA[:, 0:512], s_ones1[:, :], s_normw[:, :], start=True, stop=True),
        reads=["consts"], writes=["PA"])
    k.emit("vector", lambda e: e.tensor_copy(s_normb[:, :], PA[:, 0:512]),
           reads=["PA"], writes=["normb"])

    stP1 = ExitStack()
    xl = [stP1.enter_context(nc.sbuf_tensor(f"xln{i}", [128, c.D], F32))
          for i in range(2)]
    xb = [stP1.enter_context(nc.sbuf_tensor(f"xnb{i}", [128, c.D], BF16))
          for i in range(2)]
    dumpb = stP1.enter_context(nc.sbuf_tensor("dumpb", [128, c.D], F32))
    lncol = stP1.enter_context(nc.sbuf_tensor("lncol", [128, 4], F32))

    def ln_block(src_ap, dst_ap, p, keys):
        xk, bk = f"xl{p}", f"xb{p}"
        xlp, xbp = xl[p], xb[p]
        k.dma("sync", f"d_x{p}", (lambda e: e.dma_start(
            out=xlp[:, :], in_=src_ap)), writes=[xk])
        k.emit("vector", lambda e: e.reduce_sum(
            lncol[:, 0:1], xlp[:, :], axis=AX.X),
            reads=[xk], writes=["lncol"])
        k.emit("scalar", lambda e: e.activation(
            lncol[:, 1:2], lncol[:, 0:1], AF.Copy, scale=cst(1.0 / c.D)),
            reads=["lncol", "consts"], writes=["lncol"])
        k.emit("vector", lambda e: e.tensor_scalar(
            xlp[:, :], xlp[:, :], lncol[:, 1:2], None, ALU.subtract),
            reads=[xk, "lncol"], writes=[xk])
        k.emit("scalar", lambda e: e.activation(
            dumpb[:, :], xlp[:, :], AF.Square, accum_out=lncol[:, 2:3]),
            reads=[xk], writes=["dumpb", "lncol"])
        k.emit("scalar", lambda e: e.activation(
            lncol[:, 3:4], lncol[:, 2:3], AF.Sqrt,
            scale=cst(1.0 / c.D), bias=cst(1e-5)),
            reads=["lncol", "consts"], writes=["lncol"])
        k.emit("vector", lambda e: e.reciprocal(
            lncol[:, 3:4], lncol[:, 3:4]),
            reads=["lncol"], writes=["lncol"])
        k.emit("vector", lambda e: e.tensor_scalar(
            xbp[:, :], xlp[:, :], lncol[:, 3:4], None, ALU.mult),
            reads=[xk, "lncol"], writes=[bk])
        k.dma("sync", f"d_xw{p}", (lambda e: e.dma_start(
            out=dst_ap, in_=xbp[:, :])), reads=[bk], writes=keys)

    for tb in range(c.NTB):
        ln_block(xfull[128 * tb:128 * (tb + 1), :],
                 xn_sc[128 * tb:128 * (tb + 1), :], tb % 2, ["xn_sc"])
    for qb in range(c.NQB):
        ln_block(xq[128 * qb:128 * (qb + 1), :],
                 xnq_sc[128 * qb:128 * (qb + 1), :], qb % 2, ["xnq_sc"])
    k.fence()
    stP1.close()

    def finish_dummy():
        sfin = st.enter_context(nc.sbuf_tensor("s_dummy", [128, c.TQ], F32))
        k.emit("vector", lambda e: e.memset(sfin[:, :], 0.0), writes=["dumz"])
        k.dma("sync", "d_out", [
            (lambda e, o=o: e.dma_start(
                out=outfm[128 * o:128 * (o + 1), :], in_=sfin[:, :]))
            for o in range(c.OUTC)], reads=["dumz"], writes=["outz"])
        k._wait("gpsimd", "d_out", k.count["d_out"])
        semh = {}
        for name in sorted(set(list(k.count.keys()) + list(K.ESEM.values()))):
            semh[name] = st.enter_context(nc.semaphore(name))
        k.replay(nc, semh)
        st.close()
        return nc

    if upto <= 1:
        return finish_dummy()

    QKV_OC = c.WC // 128
    NWG = (QKV_OC + 3) // 4

    stP2 = ExitStack()
    sbp = lambda n, s, d: stP2.enter_context(nc.sbuf_tensor(n, s, d))
    s_xnf = [sbp(f"s_xnf{f}", [128, c.TILE], BF16) for f in range(c.NF)]
    s_wgF = [sbp(f"s_wgF{f}", [128, c.WC], BF16) for f in range(c.NF)]
    s_wz = [[sbp(f"s_wz{z}_{f}", [128, c.ZW], BF16) for f in range(c.NF)]
            for z in range(c.NZC)]
    s_wab = [sbp(f"s_wab{f}", [128, c.ABW], BF16) for f in range(c.NF)]
    s_projb = sbp("s_projb", [128, QKV_OC * (c.TILE + 3)], BF16)
    s_qkvs = sbp("s_qkvs", [128, QKV_OC * c.TILE], BF16)
    s_cacc = sbp("s_cacc", [128, c.TILE], F32)
    s_zsil = sbp("s_zsil", [128, c.NCH * c.VW], BF16)
    s_zg = sbp("s_zg", [128, c.VW], F32)
    s_gc = sbp("s_gc", [c.NVH, c.TILE], F32)
    s_bc = sbp("s_bc", [c.NVH, c.TILE], F32)
    s_gtm = sbp("s_gtm", [128, c.NVH], F32)
    s_btm = sbp("s_btm", [128, c.NVH], F32)
    s_G = sbp("s_G", [128, c.NVH], F32)
    s_Gfm = sbp("s_Gfm", [c.NVH, 128], F32)
    s_gam = sbp("s_gam", [128, 4 * c.NVH], F32)
    s_nbf = sbp("s_nbf", [c.NVH, c.TILE], F32)
    s_sks = sbp("s_sks", [1, 256], F32)
    s_ktf = sbp("s_ktf", [128, 128], BF16)
    s_qtf = sbp("s_qtf", [128, 128], BF16)
    s_ktm = sbp("s_ktm", [128, 128], BF16)
    s_ksq = sbp("s_ksq", [128, 256], F32)
    s_argE = sbp("s_argE", [128, 128], F32)
    s_argN = sbp("s_argN", [128, 128], F32)
    s_E = sbp("s_E", [128, 128], F32)
    s_E2 = sbp("s_E2", [128, 128], F32)
    s_es = sbp("s_es", [128, 128], F32)
    s_ntm = sbp("s_ntm", [128, 128], F32)
    s_NT = sbp("s_NT", [128, 128], BF16)
    s_N = sbp("s_N", [128, 128], BF16)
    s_TT = [sbp(f"s_TT{i}", [128, 128], BF16) for i in range(2)]
    s_P = [sbp(f"s_P{i}", [128, 128], BF16) for i in range(2)]
    s_PT = [sbp(f"s_PT{i}", [128, 128], BF16) for i in range(2)]
    s_A2T = sbp("s_A2T", [128, 128], BF16)
    s_RHS = sbp("s_RHS", [128, 128], BF16)
    s_Xbf = sbp("s_Xbf", [128, 128], BF16)
    s_Xw = sbp("s_Xw", [128, 128], BF16)
    s_Otm = sbp("s_Otm", [128, 128], F32)
    s_dump = sbp("s_dump", [128, 128], F32)
    s_oc2 = sbp("s_oc2", [128, 2], F32)
    s_S = [sbp(f"s_S{h}", [128, 128], F32) for h in range(c.NVH)]
    s_Sbf = [sbp(f"s_Sbf{h}", [128, 128], BF16) for h in range(c.NVH)]
    s_ocb = [sbp(f"s_ocb{i}", [128, 128], BF16) for i in range(2)]
    ofc = [0]

    for h in range(c.NVH):
        k.emit("vector", (lambda e, h=h: e.memset(s_S[h][:, :], 0.0)),
               writes=[f"S{h}"])
        k.emit("vector", (lambda e, h=h: e.memset(s_Sbf[h][:, :], 0.0)),
               writes=[f"Sbf{h}"])
    for oc in range(QKV_OC):
        k.emit("vector", (lambda e, oc=oc: e.memset(
            s_projb[:, oc * (c.TILE + 3):oc * (c.TILE + 3) + 3], 0.0)),
            writes=[f"projb{oc}"])

    def load_xnf(t):
        k.dma("sync", "d_xnr", [
            (lambda e, f=f, t=t: e.dma_start(
                out=s_xnf[f][:, :],
                in_=xn_sc[t * c.TILE:(t + 1) * c.TILE, 128 * f:128 * (f + 1)],
                transpose=True))
            for f in range(c.NF)],
            reads=["xn_sc"], writes=[f"xnf{f}" for f in range(c.NF)])

    def load_wg_once():
        k.dma("gpsimd", "d_wq", [
            (lambda e, f=f: e.dma_start(out=s_wgF[f][:, :], in_=wgc[f]))
            for f in range(c.NF)], writes=["wgF"])

    def load_wz(z):
        k.dma("gpsimd", f"d_wz{z}", [
            (lambda e, f=f, z=z: e.dma_start(
                out=s_wz[z][f][:, :], in_=wzc[f, z]))
            for f in range(c.NF)], writes=[f"wz{z}"])

    def load_wab():
        k.dma("gpsimd", "d_wab", [
            (lambda e, f=f: e.dma_start(
                out=s_wab[f][:, :], in_=wabc[f]))
            for f in range(c.NF)], writes=["wab"])

    PBANK = [PP0, PP1, PA, PB]
    PBK = ["PP0", "PP1", "PA", "PB"]
    xnf_keys = [f"xnf{f}" for f in range(c.NF)]

    load_xnf(0)
    load_wg_once()
    for z in range(c.NZC):
        load_wz(z)
    load_wab()

    for t in range(c.NT):
        for g in range(NWG):
            ocs = list(range(4 * g, min(4 * g + 4, QKV_OC)))
            for j, oc in enumerate(ocs):
                bank, bkey = PBANK[j], PBK[j]
                ob = oc * (c.TILE + 3)
                k.emit("tensor", [
                    (lambda e, f=f, oc=oc, bank=bank: e.matmul(
                        bank[:, 0:c.TILE],
                        s_wgF[f][:, 128 * oc:128 * (oc + 1)],
                        s_xnf[f][:, :],
                        start=(f == 0), stop=(f == c.NF - 1)))
                    for f in range(c.NF)],
                    reads=["wgF"] + xnf_keys, writes=[bkey])
                k.emit("scalar", (lambda e, bank=bank, ob=ob: e.activation(
                    s_projb[:, ob + 3:ob + 3 + c.TILE], bank[:, 0:c.TILE],
                    AF.Copy)),
                    reads=[bkey], writes=[f"projb{oc}"])
                k.emit("vector", (lambda e, oc=oc, ob=ob: e.tensor_scalar(
                    s_cacc[:, :], s_projb[:, ob:ob + c.TILE],
                    s_convw[:, 4 * oc:4 * oc + 1], None, ALU.mult)),
                    reads=[f"projb{oc}", "consts"], writes=["cacc"])
                for tap in range(1, 4):
                    k.emit("vector", (lambda e, oc=oc, ob=ob, tap=tap:
                        e.scalar_tensor_tensor(
                            s_cacc[:, :],
                            s_projb[:, ob + tap:ob + tap + c.TILE],
                            s_convw[:, 4 * oc + tap:4 * oc + tap + 1],
                            s_cacc[:, :], ALU.mult, ALU.add)),
                        reads=[f"projb{oc}", "consts", "cacc"],
                        writes=["cacc"])
                k.emit("scalar", (lambda e, oc=oc: e.activation(
                    s_qkvs[:, oc * c.TILE:(oc + 1) * c.TILE], s_cacc[:, :],
                    AF.Silu)),
                    reads=["cacc"], writes=[f"qkvs{oc}"])
                k.emit("vector", (lambda e, oc=oc, ob=ob: e.tensor_copy(
                    s_projb[:, ob:ob + 3],
                    s_projb[:, ob + c.TILE:ob + c.TILE + 3])),
                    reads=[f"projb{oc}"], writes=[f"projb{oc}"])
        if p2lvl < 12:
            continue
        for cb in range(c.NCH):
            for z in range(c.NZC):
                zpar = (cb * c.NZC + z) % 2
                bank, bkey = (PP0, "PP0") if zpar == 0 else (PP1, "PP1")
                k.emit("tensor", [
                    (lambda e, f=f, cb=cb, z=z, bank=bank: e.matmul(
                        bank[:, 0:c.ZW],
                        s_xnf[f][:, 128 * cb:128 * (cb + 1)],
                        s_wz[z][f][:, :],
                        start=(f == 0), stop=(f == c.NF - 1)))
                    for f in range(c.NF)],
                    reads=[f"wz{z}"] + xnf_keys, writes=[bkey])
                k.emit("scalar", (lambda e, cb=cb, z=z, bank=bank: e.activation(
                    s_zsil[:, cb * c.VW + c.ZW * z:cb * c.VW + c.ZW * (z + 1)],
                    bank[:, 0:c.ZW], AF.Silu)),
                    reads=[bkey], writes=["zsil"])
        for h2 in range(c.TILE // c.W2):
            co = c.W2 * h2
            k.emit("tensor", [
                (lambda e, f=f, co=co: e.matmul(
                    PA[0:c.NVH, 0:c.W2], s_wab[f][:, 0:c.NVH],
                    s_xnf[f][:, co:co + c.W2],
                    start=(f == 0), stop=(f == c.NF - 1)))
                for f in range(c.NF)],
                reads=["wab"] + xnf_keys, writes=["PA"])
            k.emit("tensor", [
                (lambda e, f=f, co=co: e.matmul(
                    PB[0:c.NVH, 0:c.W2], s_wab[f][:, c.NVH:c.ABW],
                    s_xnf[f][:, co:co + c.W2],
                    start=(f == 0), stop=(f == c.NF - 1)))
                for f in range(c.NF)],
                reads=["wab"] + xnf_keys, writes=["PB"])
            k.emit("scalar", (lambda e, co=co: e.activation(
                s_gc[:, co:co + c.W2], PA[0:c.NVH, 0:c.W2], AF.Exp,
                bias=s_dtb[:, 0:1])),
                reads=["PA", "consts"], writes=["gbc"])
            k.emit("scalar", (lambda e, co=co: e.add(
                s_gc[:, co:co + c.W2], s_gc[:, co:co + c.W2], 1.0)),
                reads=["gbc", "consts"], writes=["gbc"])
            k.emit("scalar", (lambda e, co=co: e.activation(
                s_gc[:, co:co + c.W2], s_gc[:, co:co + c.W2],
                AF.Ln)), reads=["gbc"], writes=["gbc"])
            k.emit("vector", (lambda e, co=co: e.tensor_scalar(
                s_gc[:, co:co + c.W2], s_gc[:, co:co + c.W2],
                s_nega[:, 0:1], None, ALU.mult)),
                reads=["gbc", "nega"], writes=["gbc"])
            k.emit("scalar", (lambda e, co=co: e.activation(
                s_bc[:, co:co + c.W2], PB[0:c.NVH, 0:c.W2],
                AF.Sigmoid)), reads=["PB"], writes=["gbc"])
            k.emit("vector", (lambda e, co=co: e.tensor_scalar(
                s_nbf[:, co:co + c.W2], s_bc[:, co:co + c.W2],
                -1.0, None, ALU.mult)),
                reads=["gbc"], writes=["nbf"])
        if t < c.NT - 1:
            load_xnf(t + 1)

        if p2lvl < 13:
            continue
        for cb in range(c.NCH):
            tok0 = cb * 128
            for z in range(c.NZC):
                k.emit("vector", (lambda e, cb=cb, z=z: e.tensor_tensor(
                    s_zg[:, c.ZW * z:c.ZW * (z + 1)],
                    s_zsil[:, cb * c.VW + c.ZW * z:cb * c.VW + c.ZW * (z + 1)],
                    s_normb[:, 0:c.ZW], ALU.mult)),
                    reads=["zsil", "normb"], writes=["zg"])
            k.emit("tensor", (lambda e, tok0=tok0: e.transpose(
                PB[0:128, 0:c.NVH], s_gc[:, tok0:tok0 + 128],
                s_eyef[0:c.NVH, 0:c.NVH])),
                reads=["gbc", "consts"], writes=["PB"])
            k.emit("tensor", (lambda e, tok0=tok0: e.transpose(
                PB[0:128, 64:64 + c.NVH], s_bc[:, tok0:tok0 + 128],
                s_eyef[0:c.NVH, 0:c.NVH])),
                reads=["gbc", "consts"], writes=["PB"])
            k.emit("vector", lambda e: e.tensor_copy(
                s_gtm[:, :], PB[0:128, 0:c.NVH]),
                reads=["PB"], writes=["gbtm"])
            k.emit("vector", lambda e: e.tensor_copy(
                s_btm[:, :], PB[0:128, 64:64 + c.NVH]),
                reads=["PB"], writes=["gbtm"])
            k.emit("tensor", lambda e: e.matmul(
                PB[:, 128:128 + c.NVH], s_ltri[:, :], s_gtm[:, :],
                start=True, stop=True),
                reads=["gbtm", "consts"], writes=["PB"])
            k.emit("vector", lambda e: e.tensor_copy(
                s_G[:, :], PB[:, 128:128 + c.NVH]),
                reads=["PB"], writes=["G"])
            k.emit("tensor", lambda e: e.matmul(
                PB[0:c.NVH, 256:384], s_gtm[:, :], s_ltri[:, :],
                start=True, stop=True),
                reads=["gbtm", "consts"], writes=["PB"])
            k.emit("vector", lambda e: e.tensor_copy(
                s_Gfm[:, :], PB[0:c.NVH, 256:384]),
                reads=["PB"], writes=["Gfm"])
            k.emit("tensor", lambda e: e.matmul(
                PB[:, 384:384 + c.NVH], s_onesq[:, :],
                s_gtm[:, :], start=True, stop=True),
                reads=["gbtm", "consts"], writes=["PB"])
            k.emit("vector", lambda e: e.tensor_tensor(
                s_gam[:, 0:c.NVH], PB[:, 384:384 + c.NVH], s_G[:, :],
                ALU.subtract), reads=["G", "PB"], writes=["gam"])
            k.emit("scalar", lambda e: e.activation(
                s_gam[:, 0:c.NVH], s_gam[:, 0:c.NVH], AF.Exp),
                reads=["gam"], writes=["gam"])
            k.emit("scalar", lambda e: e.activation(
                s_gam[:, c.NVH:2 * c.NVH], PB[:, 384:384 + c.NVH], AF.Exp),
                reads=["PB"], writes=["gam"])
            k.emit("scalar", lambda e: e.activation(
                s_gam[:, 2 * c.NVH:3 * c.NVH], s_G[:, :], AF.Exp),
                reads=["G"], writes=["gam"])
            k.emit("vector", lambda e: e.tensor_tensor(
                s_gam[:, 3 * c.NVH:4 * c.NVH], s_gam[:, 2 * c.NVH:3 * c.NVH],
                s_btm[:, :], ALU.mult),
                reads=["gam", "gbtm"], writes=["gam"])
            k.emit("vector", lambda e: e.tensor_scalar(
                s_gam[:, 3 * c.NVH:4 * c.NVH], s_gam[:, 3 * c.NVH:4 * c.NVH],
                -1.0, None, ALU.mult), reads=["gam"], writes=["gam"])

            if p2lvl < 14:
                continue
            for kh in range(c.NKH):
                qoc, koc = kh, c.NKH + kh
                qsl = s_qkvs[:, qoc * c.TILE + tok0:qoc * c.TILE + tok0 + 128]
                ksl = s_qkvs[:, koc * c.TILE + tok0:koc * c.TILE + tok0 + 128]
                k.emit("scalar", (lambda e, ksl=ksl: e.activation(
                    s_ksq[:, 0:128], ksl, AF.Square)),
                    reads=[f"qkvs{koc}"], writes=["ksq"])
                k.emit("scalar", (lambda e, qsl=qsl: e.activation(
                    s_ksq[:, 128:256], qsl, AF.Square)),
                    reads=[f"qkvs{qoc}"], writes=["ksq"])
                k.emit("tensor", [
                    lambda e: e.matmul(PB[0:1, 0:128], s_onesc[:, :],
                                       s_ksq[:, 0:128], start=True, stop=True),
                    lambda e: e.matmul(PB[0:1, 128:256], s_onesc[:, :],
                                       s_ksq[:, 128:256], start=True, stop=True)],
                    reads=["ksq", "consts"], writes=["PB"])
                k.emit("scalar", lambda e: e.activation(
                    s_sks[:, 0:128], PB[0:1, 0:128], AF.Sqrt,
                    bias=cst(1e-6, 1)),
                    reads=["PB", "consts"], writes=["sks"])
                k.emit("scalar", lambda e: e.activation(
                    s_sks[:, 128:256], PB[0:1, 128:256], AF.Sqrt,
                    scale=cst(128.0, 1), bias=cst(128e-6, 1)),
                    reads=["PB", "consts"], writes=["sks"])
                k.emit("vector", lambda e: e.reciprocal(
                    s_sks[:, :], s_sks[:, :]), reads=["sks"], writes=["sks"])
                k.emit("tensor", [
                    lambda e: e.matmul(PB[:, 0:128], s_ones1[:, :],
                                       s_sks[:, 0:128], start=True, stop=True),
                    lambda e: e.matmul(PB[:, 128:256], s_ones1[:, :],
                                       s_sks[:, 128:256], start=True, stop=True)],
                    reads=["sks", "consts"], writes=["PB"])
                k.emit("vector", (lambda e, ksl=ksl: e.tensor_tensor(
                    s_ktf[:, :], ksl, PB[:, 0:128], ALU.mult)),
                    reads=[f"qkvs{koc}", "PB"], writes=["ktf"])
                k.emit("vector", (lambda e, qsl=qsl: e.tensor_tensor(
                    s_qtf[:, :], qsl, PB[:, 128:256], ALU.mult)),
                    reads=[f"qkvs{qoc}", "PB"], writes=["qtf"])
                k.emit("tensor", lambda e: e.transpose(
                    PF0[:, 0:128], s_ktf[:, :], s_eyebf[:, :]),
                    reads=["ktf", "consts"], writes=["PF0"])
                k.emit("vector", lambda e: e.tensor_copy(
                    s_ktm[:, :], PF0[:, 0:128]), reads=["PF0"], writes=["ktm"])
                k.emit("tensor", lambda e: e.matmul(
                    PA[:, 0:128], s_ktf[:, :], s_ktf[:, :],
                    start=True, stop=True), reads=["ktf"], writes=["PA"])
                k.emit("tensor", lambda e: e.matmul(
                    PA[:, 128:256], s_ktf[:, :], s_qtf[:, :],
                    start=True, stop=True), reads=["ktf", "qtf"], writes=["PA"])

                if p2lvl < 15:
                    continue
                for hh in (2 * kh, 2 * kh + 1):
                    voc = 2 * c.NKH + hh
                    k.emit("tensor", (lambda e, hh=hh: e.matmul(
                        PB[:, 256:384], s_rsel[:, 128 * hh:128 * (hh + 1)],
                        s_Gfm[:, :], start=True, stop=True)),
                        reads=["Gfm", "consts"], writes=["PB"])
                    k.emit("vector", (lambda e, hh=hh: e.scalar_tensor_tensor(
                        s_argE[:, :], PB[:, 256:384], s_G[:, hh:hh + 1],
                        s_maskE[:, :], ALU.subtract, ALU.add)),
                        reads=["PB", "G", "consts"], writes=["argE"])
                    k.emit("vector", (lambda e, hh=hh: e.scalar_tensor_tensor(
                        s_argN[:, :], PB[:, 256:384], s_G[:, hh:hh + 1],
                        s_mask2[:, :], ALU.subtract, ALU.add)),
                        reads=["PB", "G", "consts"], writes=["argN"])
                    k.emit("scalar", lambda e: e.activation(
                        s_E[:, :], s_argE[:, :], AF.Exp),
                        reads=["argE"], writes=["E"])
                    k.emit("scalar", lambda e: e.activation(
                        s_E2[:, :], s_argN[:, :], AF.Exp, scale=cst(-1.0)),
                        reads=["argN", "consts"], writes=["E2"])
                    if p2lvl < 1520:
                        continue
                    k.emit("vector", lambda e: e.tensor_tensor(
                        s_A2T[:, :], s_E[:, :], PA[:, 128:256], ALU.mult),
                        reads=["E", "PA"], writes=["A2T"])
                    k.emit("vector", lambda e: e.tensor_tensor(
                        s_es[:, :], s_E[:, :], s_mask01[:, :], ALU.mult),
                        reads=["E", "consts"], writes=["es"])
                    k.emit("vector", lambda e: e.tensor_tensor(
                        s_ntm[:, :], s_es[:, :], PA[:, 0:128], ALU.mult),
                        reads=["es", "PA"], writes=["ntm"])
                    k.emit("tensor", (lambda e, hh=hh, tok0=tok0: e.matmul(
                        PB[:, 256:384], s_rsel[:, 128 * hh:128 * (hh + 1)],
                        s_nbf[:, tok0:tok0 + 128],
                        start=True, stop=True)),
                        reads=["nbf", "consts"], writes=["PB"])
                    k.emit("vector", lambda e: e.tensor_tensor(
                        s_NT[:, :], s_ntm[:, :], PB[:, 256:384], ALU.mult),
                        reads=["ntm", "PB"], writes=["NT"])
                    k.emit("vector", lambda e: e.tensor_tensor(
                        s_es[:, :], s_E2[:, :], PA[:, 0:128], ALU.mult),
                        reads=["E2", "PA"], writes=["es"])
                    k.emit("vector", (lambda e, hh=hh: e.tensor_scalar(
                        s_N[:, :], s_es[:, :],
                        s_btm[:, hh:hh + 1], -1.0,
                        ALU.mult, ALU.mult)),
                        reads=["es", "gbtm"], writes=["N"])
                    if p2lvl < 1530:
                        continue
                    k.emit("vector", lambda e: e.tensor_tensor(
                        s_TT[0][:, :], s_NT[:, :], s_eyebf[:, :], ALU.add),
                        reads=["NT", "consts"], writes=["TT0"])
                    k.emit("vector", lambda e: e.tensor_copy(
                        s_P[0][:, :], s_N[:, :]), reads=["N"], writes=["P0"])
                    k.emit("vector", lambda e: e.tensor_copy(
                        s_PT[0][:, :], s_NT[:, :]), reads=["NT"], writes=["PT0"])
                    pp = 0
                    for lv in range(1, c.NLEV + 1 if p2lvl >= 1532 else 1):
                        np_, op_ = 1 - pp, pp
                        last = (lv == c.NLEV)
                        k.emit("tensor", (lambda e, op_=op_: e.matmul(
                            PC[:, 0:128], s_PT[op_][:, :], s_P[op_][:, :],
                            start=True, stop=True)),
                            reads=[f"P{op_}", f"PT{op_}"], writes=["PC"])
                        if not last and p2lvl >= 1533:
                            k.emit("tensor", (lambda e, op_=op_: e.matmul(
                                PC[:, 128:256], s_P[op_][:, :], s_PT[op_][:, :],
                                start=True, stop=True)),
                                reads=[f"P{op_}", f"PT{op_}"], writes=["PC"])
                        k.emit("scalar", (lambda e, np_=np_: e.activation(
                            s_P[np_][:, :], PC[:, 0:128], AF.Copy)),
                            reads=["PC"], writes=[f"P{np_}"])
                        if p2lvl < 1533:
                            pp = np_
                            continue
                        if not last:
                            k.emit("vector", (lambda e, np_=np_: e.tensor_copy(
                                s_PT[np_][:, :], PC[:, 128:256])),
                                reads=["PC"], writes=[f"PT{np_}"])
                        if p2lvl < 1534:
                            pp = np_
                            continue
                        k.emit("tensor", (lambda e, np_=np_, op_=op_: e.matmul(
                            PC[:, 256:384], s_P[np_][:, :], s_TT[op_][:, :],
                            start=True, stop=True)),
                            reads=[f"P{np_}", f"TT{op_}"], writes=["PC"])
                        if p2lvl < 1535:
                            pp = np_
                            continue
                        k.emit("vector", (lambda e, np_=np_, op_=op_:
                            e.tensor_tensor(
                                s_TT[np_][:, :], PC[:, 256:384],
                                s_TT[op_][:, :], ALU.add)),
                            reads=["PC", f"TT{op_}"], writes=[f"TT{np_}"])
                        pp = np_
                    if p2lvl < 1540:
                        continue
                    k.emit("tensor", (lambda e, hh=hh: e.matmul(
                        PD[:, 0:128], s_ktf[:, :], s_Sbf[hh][:, :],
                        start=True, stop=True)),
                        reads=["ktf", f"Sbf{hh}"], writes=["PD"])
                    k.emit("tensor", (lambda e, voc=voc, tok0=tok0: e.transpose(
                        PF0[:, 128:256],
                        s_qkvs[:, voc * c.TILE + tok0:voc * c.TILE + tok0 + 128],
                        s_eyebf[:, :])),
                        reads=[f"qkvs{voc}", "consts"], writes=["PF0"])
                    k.emit("scalar", (lambda e, hh=hh: e.activation(
                        s_dump[:, :], PF0[:, 128:256], AF.Copy,
                        scale=s_btm[:, hh:hh + 1])),
                        reads=["PF0", "gbtm"], writes=["dump"])
                    k.emit("vector", (lambda e, hh=hh: e.scalar_tensor_tensor(
                        s_RHS[:, :], PD[:, 0:128],
                        s_gam[:, 3 * c.NVH + hh:3 * c.NVH + hh + 1],
                        s_dump[:, :], ALU.mult, ALU.add)),
                        reads=["PD", "gam", "dump"], writes=["RHS"])
                    if p2lvl < 1550:
                        continue
                    k.emit("tensor", (lambda e, pp=pp: e.matmul(
                        PC[:, 384:512], s_TT[pp][:, :], s_RHS[:, :],
                        start=True, stop=True)),
                        reads=[f"TT{pp}", "RHS"], writes=["PC"])
                    k.emit("scalar", lambda e: e.activation(
                        s_Xbf[:, :], PC[:, 384:512], AF.Copy),
                        reads=["PC"], writes=["Xbf"])
                    k.emit("vector", (lambda e, hh=hh: e.tensor_scalar(
                        s_Xw[:, :], PC[:, 384:512], s_gam[:, hh:hh + 1],
                        None, ALU.mult)),
                        reads=["PC", "gam"], writes=["Xw"])
                    if p2lvl < 1560:
                        continue
                    k.emit("tensor", (lambda e, hh=hh: e.matmul(
                        PD[:, 128:256], s_qtf[:, :], s_Sbf[hh][:, :],
                        start=True, stop=True)),
                        reads=["qtf", f"Sbf{hh}"], writes=["PD"])
                    k.emit("tensor", lambda e: e.matmul(
                        PD[:, 256:384], s_A2T[:, :], s_Xbf[:, :],
                        start=True, stop=True),
                        reads=["A2T", "Xbf"], writes=["PD"])
                    k.emit("scalar", (lambda e, hh=hh: e.activation(
                        s_es[:, :], PD[:, 128:256], AF.Copy,
                        scale=s_gam[:, 2 * c.NVH + hh:2 * c.NVH + hh + 1])),
                        reads=["PD", "gam"], writes=["es"])
                    k.emit("vector", lambda e: e.tensor_tensor(
                        s_Otm[:, :], s_es[:, :], PD[:, 256:384], ALU.add),
                        reads=["es", "PD"], writes=["Otm"])
                    if p2lvl < 1570:
                        continue
                    k.emit("tensor", lambda e: e.matmul(
                        PD[:, 384:512], s_ktm[:, :], s_Xw[:, :],
                        start=True, stop=True),
                        reads=["ktm", "Xw"], writes=["PD"])
                    k.emit("vector", (lambda e, hh=hh: e.scalar_tensor_tensor(
                        s_S[hh][:, :], s_S[hh][:, :],
                        s_gam[:, c.NVH + hh:c.NVH + hh + 1],
                        PD[:, 384:512], ALU.mult, ALU.add)),
                        reads=[f"S{hh}", "gam", "PD"], writes=[f"S{hh}"])
                    k.emit("scalar", (lambda e, hh=hh: e.activation(
                        s_Sbf[hh][:, :], s_S[hh][:, :], AF.Copy)),
                        reads=[f"S{hh}"], writes=[f"Sbf{hh}"])
                    if p2lvl < 1580:
                        continue
                    k.emit("scalar", lambda e: e.activation(
                        s_dump[:, :], s_Otm[:, :], AF.Square,
                        accum_out=s_oc2[:, 0:1]),
                        reads=["Otm"], writes=["dump", "oc2"])
                    k.emit("scalar", lambda e: e.activation(
                        s_oc2[:, 1:2], s_oc2[:, 0:1], AF.Sqrt,
                        scale=cst(1.0 / 128), bias=cst(1e-6)),
                        reads=["oc2", "consts"], writes=["oc2"])
                    k.emit("vector", lambda e: e.reciprocal(
                        s_oc2[:, 1:2], s_oc2[:, 1:2]),
                        reads=["oc2"], writes=["oc2"])
                    k.emit("vector", (lambda e, hh=hh: e.scalar_tensor_tensor(
                        s_Xbf[:, :], s_Otm[:, :], s_oc2[:, 1:2],
                        s_zg[:, 128 * hh:128 * (hh + 1)],
                        ALU.mult, ALU.mult)),
                        reads=["Otm", "oc2", "zg"], writes=["Xbf"])
                    k.emit("tensor", lambda e: e.transpose(
                        PF1[:, 0:128], s_Xbf[:, :], s_eyebf[:, :]),
                        reads=["Xbf", "consts"], writes=["PF1"])
                    opar = ofc[0] % 2
                    ofc[0] += 1
                    k.emit("vector", (lambda e, opar=opar: e.tensor_copy(
                        s_ocb[opar][:, :], PF1[:, 0:128])),
                        reads=["PF1"], writes=[f"ocb{opar}"])
                    k.dma("sync", f"d_of{opar}", (lambda e, hh=hh, tok0=tok0,
                                                  t=t, opar=opar: e.dma_start(
                        out=ofm_sc[hh, :, t * c.TILE + tok0:
                                   t * c.TILE + tok0 + 128],
                        in_=s_ocb[opar][:, :])),
                        reads=[f"ocb{opar}"], writes=["ofm_sc"])
    if upto <= 2:
        k.fence()
        stP2.close()
        return finish_dummy()

    k.fence()
    stP2.close()
    st25 = ExitStack()
    s_wout = [st25.enter_context(
        nc.sbuf_tensor(f"s_wout{v}", [128, c.D], BF16))
        for v in range(c.NVH)]
    s_ofm2 = [[st25.enter_context(
        nc.sbuf_tensor(f"s_ofm2{p}_{v}", [128, c.TILE], BF16))
        for v in range(c.NVH)] for p in range(2)]
    s_sfm = [st25.enter_context(
        nc.sbuf_tensor(f"s_sfm{i}", [128, c.TILE], F32)) for i in range(2)]

    st2 = ExitStack()
    s_rs = st2.enter_context(nc.sbuf_tensor("s_rs", [1, c.TQ], BF16))
    s_gus = st2.enter_context(nc.sbuf_tensor("s_gus", [1, c.TQ], BF16))

    st3 = ExitStack()
    s_gts = st3.enter_context(nc.sbuf_tensor("s_gts", [128, c.TQ], F32))
    s_gud = [st3.enter_context(nc.sbuf_tensor(f"s_gud{i}", [128, c.W4], BF16))
             for i in range(2)]
    s_xnq = [st3.enter_context(nc.sbuf_tensor(f"s_xnq{f}", [128, c.TQ], BF16))
             for f in range(c.NF)]
    NWSL = 3
    s_wgu = [[[st3.enter_context(
        nc.sbuf_tensor(f"s_wgu{p}_{m}_{f}", [128, 512], BF16))
        for f in range(c.NF)] for m in range(2)] for p in range(NWSL)]

    def load_wgu(hb, par):
        for m, wsrc in ((0, wgT), (1, wuT)):
            k.dma("gpsimd", f"d_f{m}{par}", [
                (lambda e, m=m, f=f, hb=hb, par=par, wsrc=wsrc:
                 e.dma_start(
                     out=s_wgu[par][m][f][:, :],
                     in_=wsrc[128 * f:128 * (f + 1),
                              512 * hb:512 * (hb + 1)]))
                for f in range(c.NF)], writes=[f"wgu{par}"])

    k.dma("gpsimd", "d_wo", [
        (lambda e, v=v: e.dma_start(out=s_wout[v][:, :],
                                    in_=wout[128 * v:128 * (v + 1), :]))
        for v in range(c.NVH)], writes=["woutsb"])
    k.dma("sync", "d_xnq", [
        (lambda e, f=f: e.dma_start(
            out=s_xnq[f][:, :],
            in_=xnq_sc[:, 128 * f:128 * (f + 1)], transpose=True))
        for f in range(c.NF)],
        reads=["xnq_sc"], writes=["xnq"])
    for pp_ in range(min(NWSL, c.NHB)):
        load_wgu(pp_, pp_)

    def load_ofm(t, par):
        k.dma("sync", f"d_or{par}", [
            (lambda e, v=v, t=t, par=par: e.dma_start(
                out=s_ofm2[par][v][:, :],
                in_=ofm_sc[v, :, t * c.TILE:(t + 1) * c.TILE]))
            for v in range(c.NVH)],
            reads=["ofm_sc"], writes=[f"ofm2{par}"])

    load_ofm(0, 0)
    if c.NT > 1:
        load_ofm(1, 1)
    for t in range(c.NT):
        par = t % 2
        blk = t // (c.NT // 4)
        colo = (t % (c.NT // 4)) * c.TILE
        for outc in range(c.OUTC):
            opar = outc % 2
            bank, bkey = (PC, "PC") if opar == 0 else (PD, "PD")
            k.emit("tensor", [
                (lambda e, v=v, outc=outc, par=par, bank=bank: e.matmul(
                    bank[:, 0:c.TILE],
                    s_wout[v][:, 128 * outc:128 * (outc + 1)],
                    s_ofm2[par][v][:, :],
                    start=(v == 0), stop=(v == c.NVH - 1)))
                for v in range(c.NVH)],
                reads=["woutsb", f"ofm2{par}"], writes=[bkey])
            k.emit("scalar", (lambda e, bank=bank, opar=opar: e.activation(
                s_sfm[opar][:, 0:c.TILE], bank[:, 0:c.TILE], AF.Copy)),
                reads=[bkey], writes=[f"sfm{opar}"])
            k.dma("sync", f"d_ri{opar}", (lambda e, outc=outc, blk=blk,
                                          colo=colo, opar=opar: e.dma_start(
                out=rsin[blk, 128 * outc:128 * (outc + 1),
                         colo:colo + c.TILE],
                in_=s_sfm[opar][:, 0:c.TILE])),
                reads=[f"sfm{opar}"], writes=["rsin"])
        if t + 2 < c.NT:
            load_ofm(t + 2, par)

    k.emit("gpsimd", (lambda e: e.collective_compute(
        "ReduceScatter", ALU.add,
        replica_groups=[[0, 1, 2, 3], [4, 5, 6, 7]],
        ins=[rsin.ap().opt()], outs=[rsout.ap().opt()])),
        reads=["rsin"], writes=["rsout"])

    if upto <= 3:
        return finish_dummy()

    TQH = c.TQ // c.W4
    GUPB = [(PP0, "PP0"), (PP1, "PP1"), (PA, "PA"), (PB, "PB")]
    for h2 in range(TQH):
        bank, bkey = (PC, "PC") if h2 % 2 == 0 else (PD, "PD")
        k.emit("tensor", [
            (lambda e, f=f, h2=h2, bank=bank: e.matmul(
                bank[0:1, 0:c.W4], s_onesb[:, :],
                s_xnq[f][:, c.W4 * h2:c.W4 * (h2 + 1)],
                start=(f == 0), stop=(f == c.NF - 1)))
            for f in range(c.NF)],
            reads=["xnq", "consts"], writes=[bkey])
        k.emit("vector", (lambda e, h2=h2, bank=bank: e.tensor_copy(
            s_rs[:, c.W4 * h2:c.W4 * (h2 + 1)], bank[0:1, 0:c.W4])),
            reads=[bkey], writes=["rs"])

    for hb in range(c.NHB):
        par = hb % NWSL
        for oc in range(4 if c.HID >= 512 else c.NHC):
            hc = 4 * hb + oc
            for h2 in range(TQH):
                gb, gk = GUPB[2 * (h2 % 2)]
                ub, uk = GUPB[2 * (h2 % 2) + 1]
                k.emit("tensor", [
                    (lambda e, f=f, oc=oc, h2=h2, par=par, gb=gb: e.matmul(
                        gb[:, 0:c.W4],
                        s_wgu[par][0][f][:, 128 * oc:128 * (oc + 1)],
                        s_xnq[f][:, c.W4 * h2:c.W4 * (h2 + 1)],
                        start=(f == 0), stop=False))
                    for f in range(c.NF)] + [
                    (lambda e, h2=h2, gb=gb: e.matmul(
                        gb[:, 0:c.W4], s_m128[:, :],
                        s_rs[:, c.W4 * h2:c.W4 * (h2 + 1)],
                        start=False, stop=True))],
                    reads=[f"wgu{par}", "xnq", "rs", "consts"], writes=[gk])
                k.emit("tensor", [
                    (lambda e, f=f, oc=oc, h2=h2, par=par, ub=ub: e.matmul(
                        ub[:, 0:c.W4],
                        s_wgu[par][1][f][:, 128 * oc:128 * (oc + 1)],
                        s_xnq[f][:, c.W4 * h2:c.W4 * (h2 + 1)],
                        start=(f == 0), stop=False))
                    for f in range(c.NF)] + [
                    (lambda e, h2=h2, ub=ub: e.matmul(
                        ub[:, 0:c.W4], s_m128[:, :],
                        s_rs[:, c.W4 * h2:c.W4 * (h2 + 1)],
                        start=False, stop=True))],
                    reads=[f"wgu{par}", "xnq", "rs", "consts"], writes=[uk])
                k.emit("scalar", (lambda e, hc=hc, h2=h2, gb=gb: e.activation(
                    s_gts[:, c.W4 * h2:c.W4 * (h2 + 1)], gb[:, 0:c.W4],
                    AF.Silu, scale=s_sg2[:, hc:hc + 1])),
                    reads=[gk, "consts"], writes=["gts"])
                gpar = (hc * TQH + h2) % 2
                k.emit("vector", (lambda e, hc=hc, h2=h2, ub=ub, gpar=gpar:
                    e.scalar_tensor_tensor(
                        s_gud[gpar][:, :],
                        ub[:, 0:c.W4], s_su2[:, hc:hc + 1],
                        s_gts[:, c.W4 * h2:c.W4 * (h2 + 1)],
                        ALU.mult, ALU.mult)),
                    reads=[uk, "consts", "gts"], writes=[f"gud{gpar}"])
                gusb, gusk = (PC, "PC") if h2 % 2 == 0 else (PD, "PD")
                k.emit("tensor", (lambda e, gpar=gpar, hc=hc, gusb=gusb:
                    e.matmul(
                        gusb[0:1, 0:c.W4], s_onesb[:, :], s_gud[gpar][:, :],
                        start=(hc == 0), stop=(hc == c.NHC - 1))),
                    reads=[f"gud{gpar}", "consts"], writes=[gusk])
                k.dma("sync", f"d_gw{gpar}", (lambda e, gpar=gpar, hc=hc,
                                              h2=h2: e.dma_start(
                    out=gu_sc[128 * hc:128 * (hc + 1),
                              c.W4 * h2:c.W4 * (h2 + 1)],
                    in_=s_gud[gpar][:, :])),
                    reads=[f"gud{gpar}"], writes=["gu_sc"])
        if hb + NWSL < c.NHB:
            load_wgu(hb + NWSL, par)
    for h2 in range(TQH):
        bank, bkey = (PC, "PC") if h2 % 2 == 0 else (PD, "PD")
        k.emit("vector", (lambda e, h2=h2, bank=bank: e.tensor_copy(
            s_gus[:, c.W4 * h2:c.W4 * (h2 + 1)], bank[0:1, 0:c.W4])),
            reads=[bkey], writes=["gus"])
    k.fence()
    st3.close()

    if upto <= 4:
        return finish_dummy()
    st4 = ExitStack()
    s_acc = st4.enter_context(
        nc.sbuf_tensor("s_acc", [128, c.OUTC * c.TQ], F32))
    s_wd = [[st4.enter_context(
        nc.sbuf_tensor(f"s_wd{p}_{j}", [128, c.D], BF16))
        for j in range(c.GSZ)] for p in range(2)]
    s_gur = [[st4.enter_context(
        nc.sbuf_tensor(f"s_gur{p}_{j}", [128, c.TQ], BF16))
        for j in range(c.GSZ)] for p in range(2)]
    s_fin = [st4.enter_context(
        nc.sbuf_tensor(f"s_fin{i}", [128, c.TQ], F32)) for i in range(4)]

    def load_wd(g, par):
        hcs = list(range(c.GSZ * g, min(c.GSZ * (g + 1), c.NHC)))
        k.dma("gpsimd", f"d_wd{par}", [
            (lambda e, j=j, hc=hc, par=par: e.dma_start(
                out=s_wd[par][j][:, :], in_=wdT[128 * hc:128 * (hc + 1), :]))
            for j, hc in enumerate(hcs)], writes=[f"wd{par}"])
        k.dma("sync", f"d_gr{par}", [
            (lambda e, j=j, hc=hc, par=par: e.dma_start(
                out=s_gur[par][j][:, :],
                in_=gu_sc[128 * hc:128 * (hc + 1), :]))
            for j, hc in enumerate(hcs)],
            reads=["gu_sc"], writes=[f"gur{par}"])

    load_wd(0, 0)
    for g in range(c.NGRP):
        par = g % 2
        if g + 1 < c.NGRP:
            load_wd(g + 1, 1 - par)
        hcs = list(range(c.GSZ * g, min(c.GSZ * (g + 1), c.NHC)))
        for outc in range(c.OUTC):
            for h2 in range(TQH):
                bank, bkey = GUPB[2 * (outc % 2) + h2 % 2]
                fns = [
                    (lambda e, j=j, hc=hc, outc=outc, h2=h2, par=par,
                     bank=bank, hn=len(hcs), g=g: e.matmul(
                        bank[:, 0:c.W4],
                        s_wd[par][j][:, 128 * outc:128 * (outc + 1)],
                        s_gur[par][j][:, c.W4 * h2:c.W4 * (h2 + 1)],
                        start=(j == 0),
                        stop=(j == hn - 1 and g != c.NGRP - 1)))
                    for j, hc in enumerate(hcs)]
                if g == c.NGRP - 1:
                    fns.append(lambda e, h2=h2, bank=bank: e.matmul(
                        bank[:, 0:c.W4], s_m128[:, :],
                        s_gus[:, c.W4 * h2:c.W4 * (h2 + 1)],
                        start=False, stop=True))
                k.emit("tensor", fns,
                       reads=[f"wd{par}", f"gur{par}", "gus", "consts"],
                       writes=[bkey])
                ao = outc * c.TQ + c.W4 * h2
                if g == 0:
                    k.emit("vector", (lambda e, ao=ao, bank=bank: e.tensor_copy(
                        s_acc[:, ao:ao + c.W4], bank[:, 0:c.W4])),
                        reads=[bkey], writes=[f"acc{outc}"])
                else:
                    k.emit("vector", (lambda e, ao=ao, bank=bank, outc=outc:
                        e.tensor_tensor(
                            s_acc[:, ao:ao + c.W4], bank[:, 0:c.W4],
                            s_acc[:, ao:ao + c.W4], ALU.add)),
                        reads=[bkey, f"acc{outc}"], writes=[f"acc{outc}"])

    for outc in range(c.OUTC):
        k.dma("sync", "d_ro", (lambda e, outc=outc: e.dma_start(
            out=s_fin[0][:, :], in_=rsout[128 * outc:128 * (outc + 1), :])),
            reads=["rsout"], writes=["fin0"])
        k.dma("sync", "d_xt", (lambda e, outc=outc: e.dma_start(
            out=s_fin[1][:, :], in_=xqT[128 * outc:128 * (outc + 1), :])),
            writes=["fin1"])
        k.emit("scalar", (lambda e, outc=outc: e.activation(
            s_fin[2][:, :], s_acc[:, outc * c.TQ:(outc + 1) * c.TQ], AF.Silu,
            scale=s_sd2[:, outc:outc + 1])),
            reads=[f"acc{outc}", "consts"], writes=["fin2"])
        k.emit("vector", lambda e: e.tensor_tensor(
            s_fin[3][:, :], s_fin[0][:, :], s_fin[2][:, :], ALU.mult),
            reads=["fin0", "fin2"], writes=["fin3"])
        k.emit("vector", lambda e: e.tensor_tensor(
            s_fin[3][:, :], s_fin[3][:, :], s_fin[1][:, :], ALU.add),
            reads=["fin3", "fin1"], writes=["fin3"])
        k.dma("sync", "d_out", (lambda e, outc=outc: e.dma_start(
            out=outfm[128 * outc:128 * (outc + 1), :], in_=s_fin[3][:, :])),
            reads=["fin3"], writes=[f"out{outc}"])

    k._wait("gpsimd", "d_out", k.count["d_out"])

    semh = {}
    for name in sorted(set(list(k.count.keys()) + list(K.ESEM.values()))):
        semh[name] = st.enter_context(nc.semaphore(name))
    k.replay(nc, semh)
    st4.close()
    st2.close()
    st25.close()
    st.close()
    return nc


def stage(cfg, A):
    c = cfg
    x = np.ascontiguousarray(np.asarray(A["L0_attn_hidden_states"], np.float32))
    Wqkv = np.asarray(A["L0_attn_W_qkv"], np.float32)
    Wz = np.asarray(A["L0_attn_W_z"], np.float32)
    Wb = np.asarray(A["L0_attn_W_b"], np.float32)
    Wa = np.asarray(A["L0_attn_W_a"], np.float32)
    convw = np.asarray(A["L0_attn_conv_w"], np.float32)
    dtb = np.asarray(A["L0_attn_dt_bias"], np.float32)
    alog = np.asarray(A["L0_attn_A_log"], np.float32)
    normw = np.asarray(A["L0_attn_norm_w"], np.float32)
    Wout = np.asarray(A["L0_attn_W_out"], np.float32)
    wqg = np.asarray(A["L0_ffn_wq_gate"], np.int32)
    sg = np.asarray(A["L0_ffn_s_gate"], np.float32)
    wqu = np.asarray(A["L0_ffn_wq_up"], np.int32)
    su = np.asarray(A["L0_ffn_s_up"], np.float32)
    wqd = np.asarray(A["L0_ffn_wq_down"], np.int32)
    sd = np.asarray(A["L0_ffn_s_down"], np.float32)

    KD = 4 * c.QW
    wgT = np.ascontiguousarray(wqg.T)
    wuT = np.ascontiguousarray(wqu.T)
    wdT = np.ascontiguousarray(wqd.T)
    sg2 = np.ascontiguousarray(sg.reshape(c.NHC, 128).T)
    su2 = np.ascontiguousarray(su.reshape(c.NHC, 128).T)
    sd2 = np.ascontiguousarray(sd.reshape(c.OUTC, 128).T)
    idx = np.arange(128)
    consts = dict(
        eyebf=np.eye(128, dtype=BF),
        eyef=np.eye(128, dtype=np.float32),
        ltri=np.triu(np.ones((128, 128), np.float32)),
        maskE=np.where(idx[None, :] >= idx[:, None], 0.0, -1e30).astype(np.float32),
        mask2=np.where(idx[None, :] < idx[:, None], 0.0, 1e30).astype(np.float32),
        mask01=np.triu(np.ones((128, 128), np.float32), 1),
        rowsel=np.ascontiguousarray(np.repeat(
            np.eye(cfg.NVH, dtype=np.float32), 128, axis=1)),
        normw4=np.ascontiguousarray(np.tile(normw, 512 // normw.shape[0])[None, :]).astype(np.float32),
    )
    in_maps = []
    for core in range(8):
        b, hg = core // 4, core % 4
        qsl = slice(c.QW * hg, c.QW * (hg + 1))
        ksl = slice(KD + c.QW * hg, KD + c.QW * (hg + 1))
        vsl = slice(2 * KD + c.VW * hg, 2 * KD + c.VW * (hg + 1))
        zsl = slice(c.VW * hg, c.VW * (hg + 1))
        hsl = slice(c.NVH * hg, c.NVH * (hg + 1))
        wg = np.concatenate(
            [Wqkv[:, qsl], Wqkv[:, ksl], Wqkv[:, vsl]], axis=1)
        wgc = np.ascontiguousarray(wg.reshape(c.NF, 128, c.WC))
        wzc = np.ascontiguousarray(
            Wz[:, zsl].reshape(c.NF, 128, c.NZC, c.ZW).transpose(0, 2, 1, 3))
        wab = np.concatenate([Wa[:, hsl], Wb[:, hsl]], axis=1)
        wabc = np.ascontiguousarray(wab.reshape(c.NF, 128, c.ABW))
        cw = np.concatenate([convw[qsl], convw[ksl], convw[vsl]], axis=0)
        xb = x[b]
        m = dict(
            xfull=xb,
            xq=np.ascontiguousarray(xb[c.TQ * hg:c.TQ * (hg + 1)]),
            xqT=np.ascontiguousarray(xb[c.TQ * hg:c.TQ * (hg + 1)].T),
            wgc=wgc, wzc=wzc, wabc=wabc,
            convw=np.ascontiguousarray(cw),
            alog=np.ascontiguousarray(alog[hsl][:, None]),
            dtb=np.ascontiguousarray(dtb[hsl][:, None]),
            wout=np.ascontiguousarray(Wout[c.VW * hg:c.VW * (hg + 1)]),
            wgT=wgT, wuT=wuT, wdT=wdT, sg2=sg2, su2=su2, sd2=sd2,
            **consts)
        in_maps.append(m)
    return in_maps


def assemble(cfg, outs):
    c = cfg
    out = np.empty((2, c.T, c.D), np.float32)
    for core in range(8):
        b, hg = core // 4, core % 4
        out[b, c.TQ * hg:c.TQ * (hg + 1), :] = outs[core].T
    return out


def kernel(**inputs):
    cfg = REAL
    in_maps = stage(cfg, inputs)
    nc = build(cfg)
    res = run_bass_kernel_spmd(nc, in_maps, core_ids=list(range(8)))
    return assemble(cfg, [np.asarray(r["outfm"]) for r in res.results])
